# version 09
# baseline (speedup 1.0000x reference)
import numpy as np
import ml_dtypes
from contextlib import ExitStack

import concourse.bass as bass
import concourse.mybir as mybir
from concourse.bass_utils import run_bass_kernel_spmd

F32 = mybir.dt.float32
BF16 = mybir.dt.bfloat16
I32 = mybir.dt.int32
AF = mybir.ActivationFunctionType
ALU = mybir.AluOpType
AX = mybir.AxisListType

NSEQ, T, DM, H, DH, DFF = 4, 4096, 1024, 16, 64, 4096
NT = 32
NOWN = 16
NQG = 4
CUT = 40.0
GSH = -4.0
SLOPES = [2.0 ** (-(h + 1) / 2.0) for h in range(H)]
FDH = [64, 64, 128, 128, 128, 256, 256, 256, 512, 512, 512, 512, 512, 512, 512, 512]
TWO_PI = float(2.0 * np.pi)
PI_CLAMP = float(np.nextafter(np.float32(np.pi), np.float32(0.0)))
EPS = 1e-5
import os
DBG_NT = int(os.environ.get("KDBG_NT", NT))
DBG_NQG = int(os.environ.get("KDBG_NQG", NQG))
DBG_NGG = int(os.environ.get("KDBG_NGG", 8))
DBG_FIN = int(os.environ.get("KDBG_FIN", 1))
DBG_DUMP = int(os.environ.get("KDBG_DUMP", 0))
DBG_KC = int(os.environ.get("KDBG_KC", 8))
DBG_KEV = int(os.environ.get("KDBG_KEV", 1))
DBG_A = int(os.environ.get("KDBG_A", 3))
DBG_PAIRS = int(os.environ.get("KDBG_PAIRS", 8))
DBG_A4 = int(os.environ.get("KDBG_A4", 9))
DBG_LNY = int(os.environ.get("KDBG_LNY", 0))
SIMSAFE = int(os.environ.get("KDBG_SIMSAFE", 0))
DBG_YEV = int(os.environ.get("KDBG_YEV", 1))


def band_lo(h, a):
    return max(0, 8 * a - int(np.ceil(CUT / SLOPES[h] / 128.0)))


def slot_geom(a, ktg):
    d = ktg - 8 * a
    jjs = 0 if d < 0 else min(3, d // 2)
    return jjs, jjs * 128


def exp_chunks(h, qs):
    fd = FDH[h]
    out = []
    for c in range(512 // fd):
        c0, c1 = c * fd, (c + 1) * fd
        if max(c0, qs) < c1:
            out.append((c, max(c0, qs), c1))
    return out


def build_bias_cols():
    cols = {}
    for a in range(NQG):
        for h in range(H):
            for ktg in range(band_lo(h, a), 8 * a + 8):
                jjs, qs = slot_geom(a, ktg)
                for (c, _c0, _c1) in exp_chunks(h, qs):
                    k = (h, ktg - 8 * a, c)
                    if k not in cols:
                        cols[k] = len(cols)
    return cols


BCOLS = build_bias_cols()
NB = len(BCOLS)


def host_bias_table(p):
    tab = np.zeros((128, NB), dtype=np.float32)
    kp = np.arange(128, dtype=np.float64)
    for (h, d, c), idx in BCOLS.items():
        fd = FDH[h]
        if d >= 0 and (d % 2 == 1) and p == 0 and fd <= 128:
            jjs = d // 2
            if (c * fd) // 128 == jjs and ((c + 1) * fd - 1) // 128 == jjs:
                tab[:, idx] = -1e30
                continue
        rc = c * fd + fd // 2
        jl, cl = rc // 128, rc % 128
        tab[:, idx] = (SLOPES[h] * (128.0 * d + kp - 128.0 * (2 * jl + p) - cl)
                       + GSH).astype(np.float32)
    return tab


def host_masks(p):
    kk = np.arange(128)[:, None]
    qq = np.arange(128)[None, :]
    tri = (kk <= qq).astype(np.float32)
    if p == 0:
        return tri, np.zeros((128, 128), np.float32)
    return np.ones((128, 128), np.float32), tri


CH_ALIAS = {
    "w12": "kv", "cfD": "lnrdy", "cfA": "ln1", "cfd": "lnfd",
    "ffx": "xin", "ffxf": "xfree",
    "ln2d": "lnrdyA", "ln2fd": "lqd0", "ln2free": "lqf0",
    "hidd0": "kpd0", "hidd1": "kpd1", "hidf0": "kpf0", "hidf1": "kpf1",
    "hev0": "vpd0", "hev1": "vpd1", "hfree0": "vpf0", "hfree1": "vpf1",
    "kD0": "vpd2", "kD1": "vpd3", "kP0": "vpf2", "kP1": "vpf3",
    "sinA0": "qpd0", "sinA1": "qpd1", "qD0": "qpf0", "qD1": "qpf1",
    "acd0": "scd0", "acd1": "scd1", "acf0": "scf0", "acf1": "scf1",
    "hidd2": "scd2", "hidf2": "scf2", "acd2": "scd3", "acf2": "scf3",
    "y2f0": "pbd0", "y2f1": "pbd1", "y2f2": "pbm0", "y2f3": "pbm1",
    "y2d": "avd0", "y2ev": "avd1", "y2bf": "aff0", "y2w": "aff1",
    "ln3S": "lnY", "r3A": "yev", "r3d": "ylnd",
    "fin": "xr", "finf": "x1d", "tsd": "r2d", "outd": "rif", "out": "wod0",
}


class Build:
    def __init__(self, nc, S):
        self.nc = nc
        self.S = S
        self._ch = {}

    def ch(self, name):
        name = CH_ALIAS.get(name, name)
        if name not in self._ch:
            c = type("C", (), {})()
            c.sem = self.S.enter_context(self.nc.semaphore(name))
            c.n = 0
            c.heads = {}
            self._ch[name] = c
        return self._ch[name]

    def w(self, e, *names):
        for nm in names:
            c = self.ch(nm)
            if c.n:
                e.wait_ge(c.sem, c.n)
            for (sem, v) in c.heads.values():
                e.wait_ge(sem, v)

    def inc(self, e, instr, name, k=1):
        if not hasattr(self, "_inced"):
            self._inced = {}
            self._inced_refs = []
        c = self.ch(name)
        key = id(instr)
        if key in self._inced:
            sem, v = self._inced[key]
            c.heads[id(sem)] = (sem, v)
        else:
            instr.then_inc(c.sem, k)
            c.n += k
            self._inced[key] = (c.sem, c.n)
            self._inced_refs.append(instr)

    def T(self, fn): self.blk.tensor(fn)
    def V(self, fn): self.blk.vector(fn)
    def SC(self, fn): self.blk.scalar(fn)
    def G(self, fn): self.blk.gpsimd(fn)
    def Y(self, fn): self.blk.sync(fn)


def ap3(t, off, nblk, blkstep, inner, freesz):
    return bass.AP(t, off, [[freesz, 128], [blkstep, nblk], [1, inner]])


def build_nc():
    nc = bass.Bass("TRN2", target_bir_lowering=False)
    I = {}
    for nm, shp, dt in [
        ("xkv", [T, DM], F32), ("xqo", [NOWN * 128, DM], F32),
        ("wqT", [DM, DM], F32), ("wkT", [DM, DM], F32),
        ("wvT", [DM, DM], F32), ("woT", [DM, DM], F32),
        ("w1", [DM, DFF], F32), ("w2", [DFF, DM], F32),
        ("thfm", [128, 32], F32), ("b1fm", [128, 32], F32),
        ("wrfm", [128, 32], F32), ("wifm", [128, 32], F32),
        ("b2bc", [128, DM], F32), ("btab", [128, NB + 1], F32),
        ("cbf", [128, 512], BF16),
    ]:
        I[nm] = nc.dram_tensor(nm, shp, dt, kind="ExternalInput").ap()
    yout = nc.dram_tensor("y", [NOWN * 128, DM], F32, kind="ExternalOutput").ap()
    if DBG_DUMP:
        z0 = {}
        z0["dbgk"] = nc.dram_tensor("dbgk", [8 * 128, T], BF16, kind="ExternalOutput").ap()
        z0["dbgv"] = nc.dram_tensor("dbgv", [NT * 128, DM], BF16, kind="ExternalOutput").ap()
        z0["dbgq"] = nc.dram_tensor("dbgq", [8 * 128, 512], BF16, kind="ExternalOutput").ap()
        z0["dbga"] = nc.dram_tensor("dbga", [8 * 128, 512], BF16, kind="ExternalOutput").ap()
        z0["dbgx1"] = nc.dram_tensor("dbgx1", [NOWN * 128, DM], F32, kind="ExternalOutput").ap()
        z0["dbgy2"] = nc.dram_tensor("dbgy2", [NOWN * 128, DM], F32, kind="ExternalOutput").ap()
    x1s = nc.dram_tensor("x1s", [NOWN * 128, DM], F32).ap()
    y2s = nc.dram_tensor("y2s", [NOWN * 128, DM], F32).ap()

    z = dict(I=I)
    if DBG_DUMP:
        z.update(z0)
    z["xkv_t"] = I["xkv"].rearrange("(t p) d -> t p d", p=128)
    z["xqo_t"] = I["xqo"].rearrange("(t p) d -> t p d", p=128)
    z["x1s_t"] = x1s.rearrange("(t p) d -> t p d", p=128)
    z["y2s_t"] = y2s.rearrange("(t p) d -> t p d", p=128)
    z["yout_t"] = yout.rearrange("(t p) d -> t p d", p=128)

    with ExitStack() as S:
        E = S.enter_context
        b = Build(nc, S)

        z["sc"] = [E(nc.psum_tensor(f"sc{i}", [128, 512], F32)) for i in range(3)]
        z["af"] = [E(nc.psum_tensor(f"af{i}", [128, 512], F32)) for i in range(2)]
        z["sm"] = [E(nc.psum_tensor(f"sm{i}", [128, 512], F32)) for i in range(2)]
        z["tpp"] = E(nc.psum_tensor("tpp", [128, 1024], BF16))

        z["cb"] = E(nc.sbuf_tensor("cb", [128, 512], BF16))
        z["bts"] = E(nc.sbuf_tensor("bts", [128, NB + 1], F32))
        z["xin"] = [E(nc.sbuf_tensor("xin0", [128, DM], F32))]
        z["sqs"] = E(nc.sbuf_tensor("sqs", [128, 512], F32))
        z["st4"] = E(nc.sbuf_tensor("st4", [128, 16], F32))
        z["mn1"] = E(nc.sbuf_tensor("mn1", [128, 32], F32))
        z["nv1"] = E(nc.sbuf_tensor("nv1", [128, 32], F32))
        z["rs1"] = E(nc.sbuf_tensor("rs1", [128, 32], F32))
        z["ng1"] = E(nc.sbuf_tensor("ng1", [128, 32], F32))
        z["r2o"] = E(nc.sbuf_tensor("r2o", [128, NOWN], F32))
        z["m2o"] = E(nc.sbuf_tensor("m2o", [128, NOWN], F32))
        z["nv3"] = E(nc.sbuf_tensor("nv3", [128, NOWN], F32))
        z["mn3"] = E(nc.sbuf_tensor("mn3", [128, NOWN], F32))
        z["r3o"] = E(nc.sbuf_tensor("r3o", [128, NOWN], F32))
        z["m3o"] = E(nc.sbuf_tensor("m3o", [128, NOWN], F32))
        z["lntm"] = E(nc.sbuf_tensor("lntm", [128, DM], BF16))
        z["ysb"] = E(nc.sbuf_tensor("ysb", [128, DM], F32))
        z["x1b"] = E(nc.sbuf_tensor("x1b", [128, DM], F32))

        with ExitStack() as SAo:
            EA = SAo.enter_context
            z["kfm"] = [EA(nc.sbuf_tensor(f"kfm{c}", [128, T], BF16)) for c in range(8)]
            z["vsb"] = [EA(nc.sbuf_tensor(f"vsb{t}", [128, 1040], BF16)) for t in range(NT)]

            with ExitStack() as SP:
                EP = SP.enter_context
                z["lnf"] = EP(nc.sbuf_tensor("lnf", [128, 8 * 512], BF16))
                z["wk"] = EP(nc.sbuf_tensor("wk", [128, 8 * DM], BF16))
                z["wv"] = EP(nc.sbuf_tensor("wv", [128, 8 * DM], BF16))
                with nc.Block() as blk:
                    b.blk = blk
                    emit_phaseP(b, z)
                    if DBG_DUMP:
                        def _dp(e):
                            b.w(e, "kv")
                            for c in range(8):
                                b.inc(e, e.dma_start(
                                    z["dbgk"].rearrange("(c p) t -> c p t", p=128)[c],
                                    z["kfm"][c][:]), "dmp", 16)
                            for t in range(DBG_NT):
                                b.inc(e, e.dma_start(
                                    z["dbgv"].rearrange("(t p) d -> t p d", p=128)[t],
                                    bass.AP(z["vsb"][t], 0,
                                            [[1040, 128], [65, 16], [1, 64]])), "dmp", 16)
                            dq1 = z["dbgx1"].rearrange("(t p) d -> t p d", p=128)[1]
                            dq = z["dbgx1"].rearrange("(t p) d -> t p d", p=128)[0]
                            b.inc(e, e.dma_start(dq[:, 0:32], z["mn1"][:]), "dmp", 16)
                            b.inc(e, e.dma_start(dq[:, 32:64], z["nv1"][:]), "dmp", 16)
                            b.inc(e, e.dma_start(dq[:, 64:96], z["rs1"][:]), "dmp", 16)
                            b.inc(e, e.dma_start(dq[:, 96:128], z["ng1"][:]), "dmp", 16)
                            b.w(e, "dmp")
                        b.Y(_dp)

                        def _dp2(e):
                            dq1 = z["dbgx1"].rearrange("(t p) d -> t p d", p=128)[1]
                            b.inc(e, e.dma_start(
                                dq1[:, 0:16],
                                bass.AP(z["vsb"][0], 64, [[1040, 128], [65, 16]]),
                                allow_slow_non_contiguous=True),
                                "dmp", 16)
                            b.w(e, "dmp")
                        b.G(_dp2)

                        def _dp3(e):
                            b.w(e, "dmp")
                        b.Y(_dp)

            with ExitStack() as SA:
                EA2 = SA.enter_context
                z["qbuf"] = [EA2(nc.sbuf_tensor(f"qbuf{c}", [128, 512], BF16))
                             for c in range(8)]
                z["lnqt"] = [EA2(nc.sbuf_tensor("lnqt0", [128, DM], BF16))]
                z["wq"] = EA2(nc.sbuf_tensor("wq", [128, 8 * DM], BF16))
                z["wo"] = EA2(nc.sbuf_tensor("wo", [128, 8 * DM], BF16))
                z["pb"] = [EA2(nc.sbuf_tensor(f"pb{i}", [128, 512], BF16))
                           for i in range(3)]
                z["rinv"] = EA2(nc.sbuf_tensor("rinv", [128, 512], F32))
                z["srow"] = EA2(nc.sbuf_tensor("srow", [1, 1024], BF16))
                z["lnt2"] = z["sqs"]
                z["afm"] = [EA2(nc.sbuf_tensor(f"afm{c}", [128, 512], BF16))
                            for c in range(8)]
                with nc.Block() as blk:
                    b.blk = blk
                    emit_phaseA(b, z)
                    if DBG_DUMP:
                        def _da(e):
                            b.w(e, "qrdy", "afmd", "x1w")
                            for c in range(8):
                                b.inc(e, e.dma_start(
                                    z["dbgq"].rearrange("(c p) t -> c p t", p=128)[c],
                                    z["qbuf"][c][:]), "dmp", 16)
                                b.inc(e, e.dma_start(
                                    z["dbga"].rearrange("(c p) t -> c p t", p=128)[c],
                                    z["afm"][c][:]), "dmp", 16)
                            for oi in range(4 * DBG_NQG):
                                b.inc(e, e.dma_start(
                                    z["dbgx1"].rearrange("(t p) d -> t p d", p=128)[oi],
                                    z["x1s_t"][oi]), "dmp", 16)
                            b.w(e, "dmp")
                        b.Y(_da)

        with ExitStack() as SF:
            EF = SF.enter_context
            z["w1s"] = [EF(nc.sbuf_tensor(f"w1s{d}", [128, DFF], BF16)) for d in range(8)]
            z["w2s"] = EF(nc.sbuf_tensor("w2s", [128, 32 * DM], BF16))
            z["b2s"] = EF(nc.sbuf_tensor("b2s", [128, DM], F32))
            z["ths"] = EF(nc.sbuf_tensor("ths", [128, 32], F32))
            z["b1s"] = EF(nc.sbuf_tensor("b1s", [128, 32], F32))
            z["wrs"] = EF(nc.sbuf_tensor("wrs", [128, 32], F32))
            z["wis"] = EF(nc.sbuf_tensor("wis", [128, 32], F32))
            z["cfs"] = EF(nc.sbuf_tensor("cfs", [128, 32], F32))
            z["bth"] = EF(nc.sbuf_tensor("bth", [128, 32], F32))
            z["ln2f"] = EF(nc.sbuf_tensor("ln2f", [128, 8 * 256], BF16))
            z["hsb"] = [EF(nc.sbuf_tensor(f"hsb{i}", [128, 256], F32)) for i in range(2)]
            z["asb"] = [EF(nc.sbuf_tensor(f"asb{i}", [128, 256], F32)) for i in range(2)]
            z["ksb"] = [EF(nc.sbuf_tensor(f"ksb{i}", [128, 256], I32)) for i in range(2)]
            z["rsb"] = [EF(nc.sbuf_tensor(f"rsb{i}", [128, 256], F32)) for i in range(2)]
            z["qvb"] = [EF(nc.sbuf_tensor(f"qvb{i}", [128, 256], F32)) for i in range(2)]
            z["acb"] = [EF(nc.sbuf_tensor(f"acb{i}", [128, 256], BF16)) for i in range(3)]
            z["y2b"] = [EF(nc.sbuf_tensor(f"y2b{i}", [128, DM], F32)) for i in range(2)]
            z["osb"] = [EF(nc.sbuf_tensor(f"osb{i}", [128, DM], F32)) for i in range(2)]
            with nc.Block() as blk:
                b.blk = blk
                if DBG_NQG == NQG:
                    emit_phaseF(b, z)
                    if DBG_DUMP:
                        def _df(e):
                            b.w(e, "y2w")
                            for oi in range(2 * DBG_NGG):
                                b.inc(e, e.dma_start(
                                    z["dbgy2"].rearrange("(t p) d -> t p d", p=128)[oi],
                                    z["y2s_t"][oi]), "dmp", 16)
                            b.w(e, "dmp")
                        b.Y(_df)
                    if DBG_FIN and DBG_NGG == 8:
                        emit_phaseFIN(b, z)

    return nc


def ln_stats(b, z, src, pi, done_ch, wait=(), mean_dst=None, nv_dst=None,
             rstd=True, free_ch=None):
    st4 = z["st4"]
    cs, cq = st4[:, pi:pi + 1], st4[:, 2 + pi:3 + pi]
    cm = mean_dst if mean_dst is not None else st4[:, 4 + pi:5 + pi]
    cn = nv_dst if nv_dst is not None else st4[:, 6 + pi:7 + pi]
    cr, cg = st4[:, 8 + pi:9 + pi], st4[:, 10 + pi:11 + pi]
    ct = st4[:, 12 + pi:13 + pi]
    c2 = st4[:, 14 + pi:15 + pi]

    def _(e):
        b.w(e, *wait)
        e.reduce_sum(cs, src, axis=AX.X)
        i0 = e.scalar_tensor_tensor(z["sqs"][:], src[:, 0:512], 1.0, src[:, 0:512],
                               ALU.mult, ALU.mult, accum_out=cq)
        if SIMSAFE:
            b.inc(e, i0, "sqacc")
            b.w(e, "sqacc")
        i1 = e.scalar_tensor_tensor(z["sqs"][:], src[:, 512:1024], 1.0,
                                    src[:, 512:1024], ALU.mult, ALU.mult,
                                    accum_out=c2)
        e.drain()
        e.tensor_tensor(cq, cq, c2, ALU.add)
        e.tensor_scalar(cm, cs, 1.0 / DM, None, ALU.mult)
        e.tensor_scalar(ct, cq, 1.0 / DM, None, ALU.mult)
        i = e.scalar_tensor_tensor(cn, cm, cm, ct, ALU.mult, ALU.subtract)
        b.inc(e, i, "stD")
        if not rstd:
            b.inc(e, i, done_ch)
        if free_ch:
            b.inc(e, i, free_ch)
    b.V(_)
    if not rstd:
        return

    def _(e):
        b.w(e, "stD", "w4")
        i1 = e.activation(ct, cn, AF.Ln, bias=z["bts"][:, NB:NB + 1], scale=-1.0)
        b.inc(e, i1, "actf")
        b.w(e, "actf")
        b.inc(e, e.activation(cr, ct, AF.Exp, scale=-0.5), "stA")
    b.SC(_)

    def _(e):
        b.w(e, "stA")
        b.inc(e, e.tensor_scalar(cg, cm, cr, -1.0, ALU.mult, ALU.mult), done_ch)
    b.V(_)


def emit_phaseP(b, z):
    NTP = DBG_NT
    I = z["I"]
    st4, tpp, lnf = z["st4"], z["tpp"], z["lnf"]
    vb = [z["sc"][0], z["sc"][1], z["af"][0], z["af"][1]]
    kb = [z["sm"][0], z["sm"][1]]

    def _(e):
        for src, dst in [(I["cbf"], z["cb"]), (I["btab"], z["bts"])]:
            b.inc(e, e.dma_start(dst[:], src), "w4", 16)
    b.Y(_)

    def _(e):
        for w_h, w_s in [(I["wkT"], z["wk"]), (I["wvT"], z["wv"])]:
            b.inc(e, e.dma_start(w_s[:].rearrange("p (c o) -> p c o", c=8),
                              w_h.rearrange("(c p) o -> p c o", p=128)),
                  "wkv", 16)
        for t in range(NT):
            i = e.memset(z["vsb"][t][:, 64:1040:65], 1.0)
        b.inc(e, i, "qz")
    b.G(_)

    for t in range(NTP):
        pi = t % 2

        def _(e, t=t, pi=pi):
            b.w(e, "xfree")
            b.inc(e, e.dma_start(z["xin"][0][:], z["xkv_t"][t]), "xin", 16)
        b.Y(_)

        ln_stats(b, z, z["xin"][0][:], pi, "ln1", wait=("xin",),
                 mean_dst=z["mn1"][:, t:t + 1], nv_dst=z["nv1"][:, t:t + 1],
                 rstd=False, free_ch="xfree")

    def _(e):
        b.w(e, "ln1", "w4")
        i = e.activation(z["ng1"][:, 0:NTP], z["nv1"][:, 0:NTP], AF.Ln,
                         bias=z["bts"][:, NB:NB + 1], scale=-1.0)
        b.inc(e, i, "actf")
        b.w(e, "actf")
        i = e.activation(z["rs1"][:, 0:NTP], z["ng1"][:, 0:NTP], AF.Exp,
                         scale=-0.5)
        b.inc(e, i, "r1d")
    b.SC(_)

    def _(e):
        b.w(e, "r1d")
        i = e.scalar_tensor_tensor(z["ng1"][:, 0:NTP], z["mn1"][:, 0:NTP],
                                   -1.0, z["rs1"][:, 0:NTP],
                                   ALU.mult, ALU.mult)
        b.inc(e, i, "g1d")
    b.V(_)

    for t in range(NTP):
        pi = t % 2

        def _(e, t=t, pi=pi):
            b.w(e, "xfree")
            b.inc(e, e.dma_start(z["xin"][0][:], z["xkv_t"][t]), "xin", 16)
        b.Y(_)

        def _(e, t=t, pi=pi):
            b.w(e, "xin", "g1d", "tp")
            i = e.tensor_scalar(z["lntm"][:], z["xin"][0][:],
                                z["rs1"][:, t:t + 1], z["ng1"][:, t:t + 1],
                                ALU.mult, ALU.add)
            b.inc(e, i, "lnrdy")
            b.inc(e, i, "xfree")
        b.V(_)

        def _(e, pi=pi):
            b.w(e, "lnrdy", "tpe", "w4", "wkv")
            for c in range(8):
                i = e.transpose(tpp[0:128, c * 128:(c + 1) * 128],
                                z["lntm"][:, c * 128:(c + 1) * 128],
                                z["cb"][:, 0:128])
            b.inc(e, i, "tp")
        b.T(_)

        def _(e, t=t):
            b.w(e, "tp")
            if t % 4 == 0:
                b.w(e, "lnffree")
            i = e.tensor_copy(ap3(lnf, (t % 4) * 128, 8, 512, 128, 4096),
                              ap3(tpp, 0, 8, 128, 128, 1024))
            b.inc(e, i, "lnfd")
            b.inc(e, i, "tpe")
        b.V(_)

        for half in range(2):
            bank = vb[2 * (t % 2) + half]
            bch, fch = f"vpd{2*(t%2)+half}", f"vpf{2*(t%2)+half}"

            def _(e, t=t, half=half, bank=bank, bch=bch, fch=fch):
                b.w(e, "lnfd", fch)
                for d in range(8):
                    i = e.matmul(bank[0:128, 0:512],
                                 z["lnf"][:, d * 512 + (t % 4) * 128:
                                          d * 512 + (t % 4) * 128 + 128],
                                 z["wv"][:, d * 1024 + half * 512:
                                         d * 1024 + half * 512 + 512],
                                 start=(d == 0), stop=(d == 7))
                b.inc(e, i, bch)
            b.T(_)

            def _(e, t=t, half=half, bank=bank, bch=bch, fch=fch):
                b.w(e, bch)
                i = e.tensor_copy(
                    bass.AP(z["vsb"][t], half * 520, [[1040, 128], [65, 8], [1, 64]]),
                    bass.AP(bank, 0, [[512, 128], [64, 8], [1, 64]]))
                b.inc(e, i, "kv")
                b.inc(e, i, fch)
            b.V(_)

        if t % 4 == 3:
            g = t // 4
            for c in range(DBG_KC):
                bank = kb[c % 2]
                bch, fch = f"kpd{c%2}", f"kpf{c%2}"

                def _(e, c=c, bank=bank, bch=bch, fch=fch):
                    b.w(e, fch)
                    for d in range(8):
                        i = e.matmul(bank[0:128, 0:512],
                                     z["wk"][:, d * 1024 + c * 128:
                                             d * 1024 + c * 128 + 128],
                                     z["lnf"][:, d * 512:d * 512 + 512],
                                     start=(d == 0), stop=(d == 7))
                    b.inc(e, i, bch)
                b.T(_)

                def _(e, c=c, g=g, bank=bank, bch=bch, fch=fch):
                    b.w(e, bch)
                    if DBG_KEV:
                        i = e.tensor_copy(z["kfm"][c][:, g * 512:g * 512 + 512],
                                          bank[0:128, 0:512])
                    else:
                        i = e.nop()
                    b.inc(e, i, "kv")
                    b.inc(e, i, fch)
                    if c == 7:
                        b.inc(e, i, "lnffree")
                b.V(_)


def emit_phaseA(b, z):
    I = z["I"]
    st4, tpp = z["st4"], z["tpp"]
    scr = z["sc"]
    rp = [0]
    pp = [0]

    def _(e):
        for w_h, w_s in [(I["wqT"], z["wq"]), (I["woT"], z["wo"])]:
            b.inc(e, e.dma_start(w_s[:].rearrange("p (c o) -> p c o", c=8),
                              w_h.rearrange("(c p) o -> p c o", p=128)),
                  "wA", 16)
    b.G(_)

    for a in range(DBG_NQG):
        for jj in range(4):
            oi = 4 * a + jj
            pi = jj % 2

            def _(e, oi=oi, pi=pi):
                b.w(e, "xfree")
                b.inc(e, e.dma_start(z["xin"][0][:], z["xqo_t"][oi]), "xin", 16)
            b.Y(_)

            ln_stats(b, z, z["xin"][0][:], pi, "lnA", wait=("xin", "rif", "g1d"),
                     mean_dst=z["mn1"][:, jj:jj + 1],
                     nv_dst=z["nv1"][:, jj:jj + 1], rstd=False,
                     free_ch="xfree")

        def _(e):
            b.w(e, "lnA", "w4")
            i = e.activation(z["ng1"][:, 0:4], z["nv1"][:, 0:4], AF.Ln,
                             bias=z["bts"][:, NB:NB + 1], scale=-1.0)
            b.inc(e, i, "actf")
            b.w(e, "actf")
            i = e.activation(z["rs1"][:, 0:4], z["ng1"][:, 0:4], AF.Exp,
                             scale=-0.5)
            b.inc(e, i, "r1d")
        b.SC(_)

        def _(e):
            b.w(e, "r1d")
            i = e.scalar_tensor_tensor(z["ng1"][:, 0:4], z["mn1"][:, 0:4],
                                       -1.0, z["rs1"][:, 0:4],
                                       ALU.mult, ALU.mult)
            b.inc(e, i, "g1d")
        b.V(_)

        for jj in range(4):
            oi = 4 * a + jj
            pi = jj % 2

            def _(e, oi=oi, pi=pi):
                b.w(e, "xfree")
                b.inc(e, e.dma_start(z["xin"][0][:], z["xqo_t"][oi]), "xin", 16)
            b.Y(_)

            def _(e, jj=jj, pi=pi):
                b.w(e, "xin", "g1d", "tp", "rif")
                i = e.tensor_scalar(z["lntm"][:], z["xin"][0][:],
                                    z["rs1"][:, jj:jj + 1],
                                    z["ng1"][:, jj:jj + 1],
                                    ALU.mult, ALU.add)
                b.inc(e, i, "lnrdyA")
                b.inc(e, i, "xfree")
            b.V(_)

            def _(e, pi=pi):
                b.w(e, "lnrdyA", "tpe")
                for c in range(8):
                    i = e.transpose(tpp[0:128, c * 128:(c + 1) * 128],
                                    z["lntm"][:, c * 128:(c + 1) * 128],
                                    z["cb"][:, 0:128])
                b.inc(e, i, "tp")
            b.T(_)

            def _(e, pi=pi):
                b.w(e, "tp", "lqf")
                i = e.tensor_copy(z["lnqt"][0][:], tpp[0:128, 0:1024])
                b.inc(e, i, "lqd")
                b.inc(e, i, "tpe")
            b.V(_)

            for c in range(8):
                bank = z["af"][c % 2]
                bch, fch = f"qpd{c%2}", f"qpf{c%2}"

                def _(e, c=c, pi=pi, bank=bank, bch=bch, fch=fch):
                    b.w(e, "lqd", fch, "wA", f"aff{c%2}")
                    for d in range(8):
                        i = e.matmul(bank[0:128, 0:128],
                                     z["wq"][:, d * 1024 + c * 128:
                                             d * 1024 + c * 128 + 128],
                                     z["lnqt"][0][:, d * 128:d * 128 + 128],
                                     start=(d == 0), stop=(d == 7))
                    b.inc(e, i, bch)
                b.T(_)

                def _(e, c=c, pi=pi, jj=jj, bank=bank, bch=bch, fch=fch):
                    b.w(e, bch, "qfree")
                    i = e.tensor_scalar(
                        z["qbuf"][c][:, jj * 128:jj * 128 + 128],
                        bank[0:128, 0:128], 0.125, None, ALU.mult)
                    b.inc(e, i, "qrdy")
                    b.inc(e, i, fch)
                    if c == 7:
                        b.inc(e, i, "lqf")
                b.V(_)

        ring = [z["sc"][0], z["sc"][1], z["sc"][2], z["sm"][1]]
        for c in range(DBG_PAIRS if DBG_A >= 2 else 0):
            h0, h1 = 2 * c, 2 * c + 1
            lo0, lo1 = band_lo(h0, a), band_lo(h1, a)
            slots = list(range(min(lo0, lo1), 8 * a + 8))
            nsl = len(slots)

            def sc_sec(si, c=c, lo0=lo0, slots=slots):
                ktg = slots[si]
                jjs, qs = slot_geom(a, ktg)
                in0 = ktg >= lo0
                rbA, rbB = (2 * si) % 4, (2 * si + 1) % 4

                def _(e, c=c, ktg=ktg, qs=qs, rbA=rbA, rbB=rbB, in0=in0):
                    if in0:
                        b.w(e, f"scf{rbA}")
                    b.w(e, f"scf{rbB}", "qrdy", "kv", "qz")
                    if in0:
                        i = e.matmul(ring[rbA][0:128, qs:512],
                                     z["kfm"][c][0:64, ktg * 128:ktg * 128 + 128],
                                     z["qbuf"][c][0:64, qs:512],
                                     start=True, stop=True, tile_position=(0, 0))
                        b.inc(e, i, f"scd{rbA}")
                    i = e.matmul(ring[rbB][0:128, qs:512],
                                 z["kfm"][c][64:128, ktg * 128:ktg * 128 + 128],
                                 z["qbuf"][c][64:128, qs:512],
                                 start=True, stop=True, tile_position=(64, 0))
                    b.inc(e, i, f"scd{rbB}")
                b.T(_)

            def exp_sec(si, c=c, h0=h0, h1=h1, lo0=lo0, slots=slots):
                ktg = slots[si]
                jjs, qs = slot_geom(a, ktg)
                d = ktg - 8 * a
                in0 = ktg >= lo0
                rbA, rbB = (2 * si) % 4, (2 * si + 1) % 4
                pA, pB = (2 * si) % 3, (2 * si + 1) % 3

                def _(e, h0=h0, h1=h1, d=d, qs=qs, rbA=rbA, rbB=rbB,
                      pA=pA, pB=pB, in0=in0):
                    if in0:
                        b.w(e, f"scd{rbA}", f"pbf{pA}")
                    b.w(e, f"scd{rbB}", f"pbf{pB}")
                    if in0:
                        for (cc, c0, c1) in exp_chunks(h0, qs):
                            col = BCOLS[(h0, d, cc)]
                            i = e.activation(z["pb"][pA][:, c0:c1],
                                             ring[rbA][0:128, c0:c1], AF.Exp,
                                             bias=z["bts"][:, col:col + 1])
                        b.inc(e, i, f"pbd{pA}")
                        b.inc(e, i, f"scf{rbA}")
                    for (cc, c0, c1) in exp_chunks(h1, qs):
                        col = BCOLS[(h1, d, cc)]
                        i = e.activation(z["pb"][pB][:, c0:c1],
                                         ring[rbB][0:128, c0:c1], AF.Exp,
                                         bias=z["bts"][:, col:col + 1])
                    b.inc(e, i, f"pbd{pB}")
                    b.inc(e, i, f"scf{rbB}")
                b.SC(_)

            def msk_sec(si, slots=slots, lo0=lo0):
                ktg = slots[si]
                jjs, qs = slot_geom(a, ktg)
                d = ktg - 8 * a
                if d < 0:
                    return f"pbd{(2*si)%3}", f"pbd{(2*si+1)%3}"
                in0 = ktg >= lo0
                pA, pB = (2 * si) % 3, (2 * si + 1) % 3
                mcol = 128 if d % 2 == 0 else 256

                def _(e, qs=qs, pA=pA, pB=pB, mcol=mcol, in0=in0):
                    if in0:
                        b.w(e, f"pbd{pA}")
                        i = e.tensor_mul(z["pb"][pA][:, qs:qs + 128],
                                         z["pb"][pA][:, qs:qs + 128],
                                         z["cb"][:, mcol:mcol + 128])
                        b.inc(e, i, f"pbm{pA}")
                    b.w(e, f"pbd{pB}")
                    i = e.tensor_mul(z["pb"][pB][:, qs:qs + 128],
                                     z["pb"][pB][:, qs:qs + 128],
                                     z["cb"][:, mcol:mcol + 128])
                    b.inc(e, i, f"pbm{pB}")
                b.G(_)
                return f"pbm{pA}", f"pbm{pB}"

            def av_sec(si, mchA, mchB, c=c, lo0=lo0, lo1=lo1, slots=slots):
                ktg = slots[si]
                jjs, qs = slot_geom(a, ktg)
                in0 = ktg >= lo0
                pA, pB = (2 * si) % 3, (2 * si + 1) % 3
                first0, first1 = (ktg == lo0), (ktg == lo1)
                last = (ktg == 8 * a + 7)

                def _(e, c=c, ktg=ktg, qs=qs, pA=pA, pB=pB, in0=in0,
                      first0=first0, first1=first1, last=last,
                      mchA=mchA, mchB=mchB):
                    if in0:
                        b.w(e, mchA)
                        if first0:
                            b.w(e, "aff0", "qpf0")
                        i = e.matmul(z["af"][0][0:65, qs:512],
                                     z["vsb"][ktg][:, 65 * 2 * c:65 * 2 * c + 65],
                                     z["pb"][pA][:, qs:512],
                                     start=first0, stop=last)
                        b.inc(e, i, f"pbf{pA}")
                        if last:
                            b.inc(e, i, "avd0")
                    b.w(e, mchB)
                    if first1:
                        b.w(e, "aff1", "qpf1")
                    i = e.matmul(z["af"][1][0:65, qs:512],
                                 z["vsb"][ktg][:, 65 * (2 * c + 1):65 * (2 * c + 1) + 65],
                                 z["pb"][pB][:, qs:512],
                                 start=first1, stop=last)
                    b.inc(e, i, f"pbf{pB}")
                    if last:
                        b.inc(e, i, "avd1")
                b.T(_)

            sc_sec(0)
            if nsl > 1:
                sc_sec(1)
            for si in range(nsl):
                mchA, mchB = None, None
                exp_sec(si)
                mchA, mchB = msk_sec(si)
                if si + 2 < nsl:
                    sc_sec(si + 2)
                av_sec(si, mchA, mchB)

            def _(e, c=c):
                b.w(e, "avd0", "avd1", "bcf")
                e.tensor_copy(z["srow"][0:1, 0:512], z["af"][0][64:65, 0:512])
                i = e.tensor_copy(z["srow"][0:1, 512:1024],
                                  z["af"][1][64:65, 0:512])
                b.inc(e, i, "s64d")
            b.V(_)

            def _(e, c=c):
                b.w(e, "s64d", "smf0")
                e.matmul(z["sm"][0][0:64, 0:512], z["cb"][0:1, 384:448],
                         z["srow"][0:1, 0:512], start=True, stop=True,
                         tile_position=(0, 0), skip_group_check=True)
                i = e.matmul(z["sm"][0][64:128, 0:512], z["cb"][0:1, 384:448],
                             z["srow"][0:1, 512:1024], start=True, stop=True,
                             tile_position=(0, 64), skip_group_check=True)
                b.inc(e, i, "bcd")
                b.inc(e, i, "bcf")
            b.T(_)

            def _(e, c=c):
                b.w(e, "bcd", "rif")
                e.activation(z["lnt2"][0:64, 0:512], z["sm"][0][0:64, 0:512],
                             AF.Ln)
                i1 = e.activation(z["lnt2"][64:128, 0:512],
                                  z["sm"][0][64:128, 0:512], AF.Ln)
                b.inc(e, i1, "actf")
                b.w(e, "actf")
                i = e.activation(z["rinv"][:, 0:512], z["lnt2"][:, 0:512],
                                 AF.Exp, scale=-1.0)
                b.inc(e, i, "rin")
                b.inc(e, i, "smf0")
            b.SC(_)

            def _(e, c=c):
                b.w(e, "rin", "afmf")
                e.tensor_tensor(z["afm"][c][0:64, :], z["af"][0][0:64, 0:512],
                                z["rinv"][0:64, 0:512], ALU.mult)
                i = e.tensor_tensor(z["afm"][c][64:128, :],
                                    z["af"][1][0:64, 0:512],
                                    z["rinv"][64:128, 0:512], ALU.mult)
                b.inc(e, i, "afmd")
                b.inc(e, i, "aff0")
                b.inc(e, i, "aff1")
                b.inc(e, i, "rif")
                if c == DBG_PAIRS - 1:
                    b.inc(e, i, "qfree")
            b.V(_)



        for jj in range(4 if (DBG_A >= 3 and DBG_PAIRS == 8) else 0):
            oi = 4 * a + jj
            pi = jj % 2
            banks = [z["sm"][0], z["sm"][1]] if pi == 0 else [z["af"][0], z["af"][1]]
            bch = f"wod{pi}"
            fch0 = ("smf0", "scf3") if pi == 0 else ("aff0", "aff1")

            if DBG_A4 >= 1:
                def _(e, oi=oi, pi=pi):
                    b.w(e, "xfree")
                    b.inc(e, e.dma_start(z["xin"][0][:], z["xqo_t"][oi]), "xr", 16)
                b.Y(_)
            if DBG_A4 < 2:
                continue

            def _(e, jj=jj, banks=banks, bch=bch, fch0=fch0):
                b.w(e, "afmd", *fch0)
                for half in range(2):
                    for cc in range(8):
                        i = e.matmul(banks[half][0:128, 0:512],
                                     z["afm"][cc][:, jj * 128:jj * 128 + 128],
                                     z["wo"][:, cc * 1024 + half * 512:
                                             cc * 1024 + half * 512 + 512],
                                     start=(cc == 0), stop=(cc == 7))
                b.inc(e, i, bch)
            b.T(_)
            if DBG_A4 < 3:
                continue

            if DBG_YEV == 0:
                def _(e, jj=jj, banks=banks, bch=bch, fch0=fch0):
                    b.w(e, bch, "yfree")
                    e.activation(z["ysb"][:, 0:512], banks[0][0:128, 0:512],
                                 AF.Identity)
                    i = e.activation(z["ysb"][:, 512:1024],
                                     banks[1][0:128, 0:512], AF.Identity)
                    b.inc(e, i, "yev")
                    b.inc(e, i, fch0[0])
                    b.inc(e, i, fch0[1])
                    if jj == 3:
                        b.inc(e, i, "afmf")
                b.SC(_)
            else:
                def _(e, jj=jj, banks=banks, bch=bch, fch0=fch0):
                    b.w(e, bch, "yfree")
                    e.tensor_copy(z["ysb"][:, 0:512], banks[0][0:128, 0:512])
                    i = e.tensor_copy(z["ysb"][:, 512:1024],
                                      banks[1][0:128, 0:512])
                    b.inc(e, i, "yev")
                    b.inc(e, i, fch0[0])
                    b.inc(e, i, fch0[1])
                    if jj == 3:
                        b.inc(e, i, "afmf")
                b.V(_)
            if DBG_A4 < 4:
                continue

            if DBG_LNY == 0:
                ln_stats(b, z, z["ysb"][:], pi, "lnY", wait=("yev",))
            elif DBG_LNY == 1:
                ln_stats(b, z, z["ysb"][:], pi, "lnY", wait=("yev",), rstd=False)
            elif DBG_LNY == 2:
                ln_stats(b, z, z["xin"][0][:], pi, "lnY", wait=("yev",))
            elif DBG_LNY == 3:
                ln_stats(b, z, z["ysb"][:], pi, "lnYx", wait=("yev",), rstd=False)
                def _f3(e, pi=pi):
                    b.w(e, "lnYx")
                    e.activation(z["st4"][:, 12 + pi:13 + pi],
                                 z["st4"][:, 6 + pi:7 + pi], AF.Ln,
                                 bias=z["bts"][:, NB:NB + 1], scale=-1.0)
                    e.drain()
                    i = e.activation(z["st4"][:, 8 + pi:9 + pi],
                                     z["st4"][:, 12 + pi:13 + pi], AF.Exp,
                                     scale=-0.5)
                    b.inc(e, i, "lnY")
                b.SC(_f3)
            elif DBG_LNY == 4:
                ln_stats(b, z, z["ysb"][:], pi, "lnYx", wait=("yev",), rstd=False)
                def _f4(e, pi=pi):
                    b.w(e, "lnYx")
                    i = e.tensor_scalar(z["st4"][:, 10 + pi:11 + pi],
                                        z["st4"][:, 4 + pi:5 + pi],
                                        z["st4"][:, 8 + pi:9 + pi], -1.0,
                                        ALU.mult, ALU.mult)
                    b.inc(e, i, "lnY")
                b.V(_f4)
            if DBG_A4 < 5:
                continue

            def _(e, pi=pi):
                b.w(e, "lnY")
                i = e.tensor_scalar(z["ysb"][:], z["ysb"][:],
                                    st4[:, 8 + pi:9 + pi],
                                    st4[:, 10 + pi:11 + pi],
                                    ALU.mult, ALU.add)
                b.inc(e, i, "ylnd")
            b.V(_)
            if DBG_A4 < 6:
                continue

            def _(e, pi=pi):
                b.w(e, "ylnd", "xr", "x1w", "ln2S")
                i = e.tensor_add(z["x1b"][:], z["ysb"][:], z["xin"][0][:])
                b.inc(e, i, "x1d")
            b.G(_)

            def _(e):
                b.w(e, "x1d")
                b.inc(e, e.nop(), "xfree")
                b.inc(e, e.nop(), "yfree")
            b.V(_)
            if DBG_A4 < 7:
                continue

            ln_stats(b, z, z["x1b"][:], pi, "ln2S", wait=("x1d",))
            if DBG_A4 < 8:
                continue

            def _(e, oi=oi, pi=pi):
                b.w(e, "ln2S")
                e.tensor_copy(z["r2o"][:, oi:oi + 1], st4[:, 8 + pi:9 + pi])
                i = e.tensor_copy(z["m2o"][:, oi:oi + 1],
                                  st4[:, 10 + pi:11 + pi])
                b.inc(e, i, "r2d")
            b.V(_)
            if DBG_A4 < 9:
                continue

            def _(e, oi=oi):
                b.w(e, "r2d")
                b.inc(e, e.dma_start(z["x1s_t"][oi], z["x1b"][:]), "x1w", 16)
            b.Y(_)


def emit_phaseF(b, z):
    I = z["I"]
    st4, tpp = z["st4"], z["tpp"]
    hid = [z["sc"][0], z["sc"][1], z["sc"][2]]
    y2bank = [[z["af"][0], z["af"][1]], [z["sm"][0], z["sm"][1]]]
    NF = DFF // 128

    def _(e):
        for d in range(8):
            src = I["w1"].rearrange("(c p) f -> c p f", p=128)[d]
            b.inc(e, e.dma_start(z["w1s"][d][:], src), "w12g", 16)
        b.inc(e, e.dma_start(z["w2s"][:].rearrange("p (c o) -> p c o", c=32),
                          I["w2"].rearrange("(c p) o -> p c o", p=128)),
              "w12g", 16)
    b.G(_)

    def _(e):
        b.inc(e, e.dma_start(z["b2s"][:], I["b2bc"]), "xr", 16)
        for src, dst in [(I["thfm"], z["ths"]), (I["b1fm"], z["b1s"]),
                         (I["wrfm"], z["wrs"]), (I["wifm"], z["wis"])]:
            b.inc(e, e.dma_start(dst[:], src), "xr", 16)
    b.Y(_)

    def _(e):
        b.w(e, "xr")
        k32 = z["ksb"][0][:, 0:32]
        e.tensor_scalar(k32, z["ths"][:], 1.0 / TWO_PI, None, ALU.mult)
        e.scalar_tensor_tensor(z["rsb"][0][:, 0:32], k32, -TWO_PI, z["ths"][:],
                               ALU.mult, ALU.add)
        e.tensor_scalar(z["rsb"][0][:, 32:64], z["rsb"][0][:, 0:32],
                        PI_CLAMP, -PI_CLAMP, ALU.min, ALU.max)
        e.tensor_scalar(z["asb"][0][:, 0:32], z["ths"][:], 1.0,
                        float(np.pi / 2), ALU.mult, ALU.add)
        e.tensor_scalar(k32, z["asb"][0][:, 0:32], 1.0 / TWO_PI, None, ALU.mult)
        e.scalar_tensor_tensor(z["rsb"][0][:, 64:96], k32, -TWO_PI,
                               z["asb"][0][:, 0:32], ALU.mult, ALU.add)
        i = e.tensor_scalar(z["rsb"][0][:, 96:128], z["rsb"][0][:, 64:96],
                            PI_CLAMP, -PI_CLAMP, ALU.min, ALU.max)
        b.inc(e, i, "cfD")
    b.V(_)

    def _(e):
        b.w(e, "cfD")
        e.activation(z["asb"][1][:, 0:32], z["rsb"][0][:, 32:64], AF.Sin)
        i = e.activation(z["asb"][1][:, 32:64], z["rsb"][0][:, 96:128], AF.Sin)
        b.inc(e, i, "cfA")
    b.SC(_)

    def _(e):
        b.w(e, "cfA")
        e.tensor_tensor(z["qvb"][0][:, 0:32], z["asb"][1][:, 32:64],
                        z["wrs"][:], ALU.mult)
        e.tensor_tensor(z["qvb"][0][:, 32:64], z["asb"][1][:, 0:32],
                        z["wis"][:], ALU.mult)
        e.tensor_tensor(z["qvb"][0][:, 64:96], z["qvb"][0][:, 0:32],
                        z["qvb"][0][:, 32:64], ALU.add)
        e.tensor_tensor(z["qvb"][0][:, 96:128], z["qvb"][0][:, 64:96],
                        z["wis"][:], ALU.mult)
        e.tensor_scalar(z["cfs"][:], z["qvb"][0][:, 96:128], 0.1, None,
                        ALU.mult)
        i = e.scalar_tensor_tensor(z["bth"][:], z["b1s"][:], 0.1, z["ths"][:],
                                   ALU.mult, ALU.add)
        b.inc(e, i, "cfd")
    b.V(_)

    for gg in range(DBG_NGG):
        for k in range(2):
            oi = 2 * gg + k

            def _(e, oi=oi, k=k):
                b.w(e, "ffxf")
                b.inc(e, e.dma_start(z["xin"][0][:], z["x1s_t"][oi]), "ffx", 16)
            b.Y(_)

            def _(e, oi=oi, k=k):
                b.w(e, "ffx", "tp")
                i = e.tensor_scalar(z["lntm"][:], z["xin"][0][:],
                                    z["r2o"][:, oi:oi + 1],
                                    z["m2o"][:, oi:oi + 1], ALU.mult, ALU.add)
                b.inc(e, i, "ln2d")
                b.inc(e, i, "ffxf")
            b.V(_)

            def _(e, k=k):
                b.w(e, "ln2d", "tpe")
                for c in range(8):
                    i = e.transpose(tpp[0:128, c * 128:(c + 1) * 128],
                                    z["lntm"][:, c * 128:(c + 1) * 128],
                                    z["cb"][:, 0:128])
                b.inc(e, i, "tp")
            b.T(_)

            def _(e, k=k):
                b.w(e, "tp")
                if k == 0:
                    b.w(e, "ln2free")
                i = e.tensor_copy(ap3(z["ln2f"], k * 128, 8, 256, 128, 2048),
                                  ap3(tpp, 0, 8, 128, 128, 1024))
                b.inc(e, i, "ln2fd")
                b.inc(e, i, "tpe")
            b.V(_)

        def hid_sec(f):
            hb = f % 3

            def _(e, f=f, hb=hb):
                b.w(e, "ln2fd", f"hidf{hb}", "w12g", "xr", "cfd")
                for dd in range(8):
                    i = e.matmul(hid[hb][0:128, 0:256],
                                 z["w1s"][dd][:, f * 128:f * 128 + 128],
                                 z["ln2f"][:, dd * 256:dd * 256 + 256],
                                 start=(dd == 0), stop=(dd == 7))
                if f == NF - 1:
                    b.inc(e, i, "ln2free")
                b.inc(e, i, f"hidd{hb}")
            b.T(_)

        hid_sec(0)
        hid_sec(1)
        hid_sec(2)
        for f in range(NF):
            hb, sp, ab = f % 3, f % 2, f % 3

            def _(e, f=f, hb=hb, sp=sp):
                b.w(e, f"hidd{hb}", f"sinA{sp}")
                e.tensor_scalar(z["asb"][sp][:], hid[hb][0:128, 0:256], 0.1,
                                z["bth"][:, f:f + 1], ALU.mult, ALU.add)
                e.tensor_scalar(z["ksb"][sp][:], z["asb"][sp][:],
                                1.0 / TWO_PI, None, ALU.mult)
                i = e.scalar_tensor_tensor(z["rsb"][sp][:], z["ksb"][sp][:],
                                           -TWO_PI, z["asb"][sp][:],
                                           ALU.mult, ALU.add)
                b.inc(e, i, f"kD{sp}")
            b.V(_)

            def _(e, sp=sp):
                b.w(e, f"kD{sp}")
                i = e.activation(z["asb"][sp][:], z["rsb"][sp][:], AF.Sin)
                b.inc(e, i, f"sinA{sp}")
            b.SC(_)

            def _(e, f=f, hb=hb, sp=sp, ab=ab):
                b.w(e, f"sinA{sp}", f"acf{ab}")
                i = e.scalar_tensor_tensor(z["qvb"][sp][:], z["asb"][sp][:],
                                           z["cfs"][:, f:f + 1],
                                           hid[hb][0:128, 0:256],
                                           ALU.mult, ALU.add)
                b.inc(e, i, f"hidf{hb}")
                i = e.tensor_scalar(z["acb"][ab][:], z["qvb"][sp][:],
                                    z["b1s"][:, f:f + 1], 0.0,
                                    ALU.add, ALU.max)
                b.inc(e, i, f"acd{ab}")
            b.V(_)

            def _(e, f=f, ab=ab):
                b.w(e, f"acd{ab}")
                if f == 0:
                    b.w(e, "y2f0", "y2f1", "y2f2", "y2f3")
                for k in range(2):
                    for half in range(2):
                        i = e.matmul(y2bank[k][half][0:128, 0:512],
                                     z["acb"][ab][:, k * 128:k * 128 + 128],
                                     z["w2s"][:, f * 1024 + half * 512:
                                             f * 1024 + half * 512 + 512],
                                     start=(f == 0), stop=(f == NF - 1))
                b.inc(e, i, f"acf{ab}")
                if f == NF - 1:
                    b.inc(e, i, "y2d")
            b.T(_)

            if f + 3 < NF:
                hid_sec(f + 3)

        for k in range(2):
            oi = 2 * gg + k

            def _(e, k=k):
                b.w(e, "y2d", "y2bf")
                for half in range(2):
                    i = e.tensor_tensor(
                        z["y2b"][k][:, half * 512:half * 512 + 512],
                        y2bank[k][half][0:128, 0:512],
                        z["b2s"][:, half * 512:half * 512 + 512], ALU.add)
                b.inc(e, i, "y2ev")
                b.inc(e, i, f"y2f{2*k}")
                b.inc(e, i, f"y2f{2*k+1}")
            b.V(_)

            ln_stats(b, z, z["y2b"][k][:], k, "ln3S", wait=("y2ev",),
                     mean_dst=z["mn3"][:, oi:oi + 1],
                     nv_dst=z["nv3"][:, oi:oi + 1], rstd=False)

            def _(e, k=k, oi=oi):
                b.w(e, "ln3S")
                b.inc(e, e.dma_start(z["y2s_t"][oi], z["y2b"][k][:]), "y2w", 16)
            b.Y(_)

        def _(e):
            b.w(e, "y2w")
            b.inc(e, e.nop(), "y2bf")
        b.Y(_)


def emit_phaseFIN(b, z):
    def _(e):
        b.w(e, "ln3S")
        i1 = e.activation(z["m3o"][:], z["nv3"][:], AF.Ln,
                          bias=z["bts"][:, NB:NB + 1], scale=-1.0)
        b.inc(e, i1, "actf")
        b.w(e, "actf")
        i = e.activation(z["r3o"][:], z["m3o"][:], AF.Exp, scale=-0.5)
        b.inc(e, i, "r3A")
    b.SC(_)

    def _(e):
        b.w(e, "r3A")
        i = e.tensor_tensor(z["m3o"][:], z["mn3"][:], z["r3o"][:], ALU.mult)
        b.inc(e, i, "r3d")
    b.V(_)

    for oi in range(NOWN):
        k = oi % 2

        def _(e, oi=oi, k=k):
            b.w(e, "finf")
            b.inc(e, e.dma_start(z["xin"][0][:], z["x1s_t"][oi]), "fin", 16)
            b.inc(e, e.dma_start(z["y2b"][k][:], z["y2s_t"][oi]), "fin", 16)
        b.Y(_)

        def _(e, oi=oi, k=k):
            b.w(e, "fin", "r3d", "outd")
            i = e.tensor_scalar(z["ysb"][:], z["y2b"][k][:],
                                z["r3o"][:, oi:oi + 1], z["m3o"][:, oi:oi + 1],
                                ALU.mult, ALU.subtract)
            b.inc(e, i, "tsd")
        b.V(_)

        def _(e, oi=oi, k=k):
            b.w(e, "tsd", "out")
            i = e.tensor_add(z["osb"][k][:], z["ysb"][:], z["xin"][0][:])
            b.inc(e, i, "outd")
            b.inc(e, i, "finf")
        b.G(_)

        def _(e, oi=oi, k=k):
            b.w(e, "outd")
            b.inc(e, e.dma_start(z["yout_t"][oi], z["osb"][k][:]), "out", 16)
        b.Y(_)

    def _(e):
        b.w(e, "out")
    b.Y(_)


_NC_CACHE = [None]


def kernel(L0_attn_x, L0_attn_Wq, L0_attn_Wk, L0_attn_Wv, L0_attn_Wo,
           L0_ffn_W1, L0_ffn_b1, L0_ffn_quantum_phases,
           L0_ffn_quantum_weights_real, L0_ffn_quantum_weights_imag,
           L0_ffn_W2, L0_ffn_b2):
    X = np.ascontiguousarray(np.asarray(L0_attn_x, dtype=np.float32))
    wqT = np.ascontiguousarray(np.asarray(L0_attn_Wq, np.float32).T)
    wkT = np.ascontiguousarray(np.asarray(L0_attn_Wk, np.float32).T)
    wvT = np.ascontiguousarray(np.asarray(L0_attn_Wv, np.float32).T)
    woT = np.ascontiguousarray(np.asarray(L0_attn_Wo, np.float32).T)
    w1 = np.ascontiguousarray(np.asarray(L0_ffn_W1, np.float32))
    w2 = np.ascontiguousarray(np.asarray(L0_ffn_W2, np.float32))
    b1 = np.asarray(L0_ffn_b1, np.float32)
    b2 = np.asarray(L0_ffn_b2, np.float32)
    th = np.asarray(L0_ffn_quantum_phases, np.float32)
    wr = np.asarray(L0_ffn_quantum_weights_real, np.float32)
    wi = np.asarray(L0_ffn_quantum_weights_imag, np.float32)

    fm = lambda v: np.ascontiguousarray(v.reshape(32, 128).T)
    b2bc = np.ascontiguousarray(np.broadcast_to(b2[None, :], (128, DM)))

    ident = np.eye(128, dtype=np.float32)
    cbf, btabs = {}, {}
    for p in range(2):
        me, mo = host_masks(p)
        cbx = np.zeros((128, 512), dtype=np.float32)
        cbx[:, 0:128] = ident
        cbx[:, 128:256] = me
        cbx[:, 256:384] = mo
        cbx[:, 384:512] = 1.0
        cbf[p] = cbx.astype(ml_dtypes.bfloat16)
        bt = host_bias_table(p)
        btabs[p] = np.ascontiguousarray(
            np.concatenate([bt, np.full((128, 1), EPS, np.float32)], axis=1))

    in_maps = []
    for core in range(8):
        s, p = core // 2, core % 2
        xs = X[s]
        own = np.ascontiguousarray(
            xs.reshape(NT, 128, DM)[p::2].reshape(NOWN * 128, DM))
        in_maps.append({
            "xkv": xs, "xqo": own,
            "wqT": wqT, "wkT": wkT, "wvT": wvT, "woT": woT,
            "w1": w1, "w2": w2,
            "thfm": fm(th), "b1fm": fm(b1), "wrfm": fm(wr), "wifm": fm(wi),
            "b2bc": b2bc, "btab": btabs[p], "cbf": cbf[p],
        })

    if _NC_CACHE[0] is None:
        _NC_CACHE[0] = build_nc()
    res = run_bass_kernel_spmd(_NC_CACHE[0], in_maps, core_ids=list(range(8)))

    out = np.empty((NSEQ, T, DM), dtype=np.float32)
    for core in range(8):
        s, p = core // 2, core % 2
        yc = np.asarray(res.results[core]["y"]).reshape(NOWN, 128, DM)
        out[s].reshape(NT, 128, DM)[p::2] = yc
    return out


if __name__ == "__main__":
    print(f"NB={NB}")
    print("building...")
    nc = build_nc()
    print("built ok")


# version 10 (final)
# speedup vs baseline: 1.1803x; 1.1803x over previous
import numpy as np
import ml_dtypes
from contextlib import ExitStack

import concourse.bass as bass
import concourse.mybir as mybir
from concourse.bass_utils import run_bass_kernel_spmd

F32 = mybir.dt.float32
BF16 = mybir.dt.bfloat16
I32 = mybir.dt.int32
AF = mybir.ActivationFunctionType
ALU = mybir.AluOpType
AX = mybir.AxisListType

NSEQ, T, DM, H, DH, DFF = 4, 4096, 1024, 16, 64, 4096
NT = 32
NOWN = 16
NQG = 4
CUT = 40.0
GSH = -4.0
SLOPES = [2.0 ** (-(h + 1) / 2.0) for h in range(H)]
FDH = [64, 64, 128, 128, 128, 256, 256, 256, 512, 512, 512, 512, 512, 512, 512, 512]
TWO_PI = float(2.0 * np.pi)
PI_CLAMP = float(np.nextafter(np.float32(np.pi), np.float32(0.0)))
EPS = 1e-5
import os
DBG_NT = int(os.environ.get("KDBG_NT", NT))
DBG_NQG = int(os.environ.get("KDBG_NQG", NQG))
DBG_NGG = int(os.environ.get("KDBG_NGG", 8))
DBG_FIN = int(os.environ.get("KDBG_FIN", 1))
DBG_DUMP = int(os.environ.get("KDBG_DUMP", 0))
DBG_KC = int(os.environ.get("KDBG_KC", 8))
DBG_KEV = int(os.environ.get("KDBG_KEV", 1))
DBG_A = int(os.environ.get("KDBG_A", 3))
DBG_PAIRS = int(os.environ.get("KDBG_PAIRS", 8))
DBG_A4 = int(os.environ.get("KDBG_A4", 9))
DBG_LNY = int(os.environ.get("KDBG_LNY", 0))
SIMSAFE = int(os.environ.get("KDBG_SIMSAFE", 0))
DBG_YEV = int(os.environ.get("KDBG_YEV", 1))


def band_lo(h, a):
    return max(0, 8 * a - int(np.ceil(CUT / SLOPES[h] / 128.0)))


def slot_geom(a, ktg):
    d = ktg - 8 * a
    jjs = 0 if d < 0 else min(3, d // 2)
    return jjs, jjs * 128


def exp_chunks(h, qs):
    fd = FDH[h]
    out = []
    for c in range(512 // fd):
        c0, c1 = c * fd, (c + 1) * fd
        if max(c0, qs) < c1:
            out.append((c, max(c0, qs), c1))
    return out


def build_bias_cols():
    cols = {}
    for a in range(NQG):
        for h in range(H):
            for ktg in range(band_lo(h, a), 8 * a + 8):
                jjs, qs = slot_geom(a, ktg)
                for (c, _c0, _c1) in exp_chunks(h, qs):
                    k = (h, ktg - 8 * a, c)
                    if k not in cols:
                        cols[k] = len(cols)
    return cols


BCOLS = build_bias_cols()
NB = len(BCOLS)


def host_bias_table(p):
    tab = np.zeros((128, NB), dtype=np.float32)
    kp = np.arange(128, dtype=np.float64)
    for (h, d, c), idx in BCOLS.items():
        fd = FDH[h]
        if d >= 0 and (d % 2 == 1) and p == 0 and fd <= 128:
            jjs = d // 2
            if (c * fd) // 128 == jjs and ((c + 1) * fd - 1) // 128 == jjs:
                tab[:, idx] = -1e30
                continue
        rc = c * fd + fd // 2
        jl, cl = rc // 128, rc % 128
        tab[:, idx] = (SLOPES[h] * (128.0 * d + kp - 128.0 * (2 * jl + p) - cl)
                       + GSH).astype(np.float32)
    return tab


def host_masks(p):
    kk = np.arange(128)[:, None]
    qq = np.arange(128)[None, :]
    tri = (kk <= qq).astype(np.float32)
    if p == 0:
        return tri, np.zeros((128, 128), np.float32)
    return np.ones((128, 128), np.float32), tri


CH_ALIAS = {
    "w12": "kv", "cfD": "lnrdy", "cfA": "ln1", "cfd": "lnfd",
    "ffx": "xin", "ffxf": "xfree",
    "ln2d": "lnrdyA", "ln2fd": "lqd0", "ln2free": "lqf0",
    "hidd0": "kpd0", "hidd1": "kpd1", "hidf0": "kpf0", "hidf1": "kpf1",
    "hev0": "vpd0", "hev1": "vpd1", "hfree0": "vpf0", "hfree1": "vpf1",
    "kD0": "vpd2", "kD1": "vpd3", "kP0": "vpf2", "kP1": "vpf3",
    "sinA0": "qpd0", "sinA1": "qpd1", "qD0": "qpf0", "qD1": "qpf1",
    "acd0": "scd0", "acd1": "scd1", "acf0": "scf0", "acf1": "scf1",
    "hidd2": "scd2", "hidf2": "scf2", "acd2": "scd3", "acf2": "scf3",
    "y2f0": "pbd0", "y2f1": "pbd1", "y2f2": "pbm0", "y2f3": "pbm1",
    "y2d": "avd0", "y2ev": "avd1", "y2bf": "aff0", "y2w": "aff1",
    "ln3S": "lnY", "r3A": "yev", "r3d": "ylnd",
    "fin": "xr", "finf": "x1d", "tsd": "r2d", "outd": "rif", "out": "wod0",
}


class Build:
    def __init__(self, nc, S):
        self.nc = nc
        self.S = S
        self._ch = {}

    def ch(self, name):
        name = CH_ALIAS.get(name, name)
        if name not in self._ch:
            c = type("C", (), {})()
            c.sem = self.S.enter_context(self.nc.semaphore(name))
            c.n = 0
            c.heads = {}
            self._ch[name] = c
        return self._ch[name]

    def w(self, e, *names):
        for nm in names:
            c = self.ch(nm)
            if c.n:
                e.wait_ge(c.sem, c.n)
            for (sem, v) in c.heads.values():
                e.wait_ge(sem, v)

    def inc(self, e, instr, name, k=1):
        if not hasattr(self, "_inced"):
            self._inced = {}
            self._inced_refs = []
        c = self.ch(name)
        key = id(instr)
        if key in self._inced:
            sem, v = self._inced[key]
            c.heads[id(sem)] = (sem, v)
        else:
            instr.then_inc(c.sem, k)
            c.n += k
            self._inced[key] = (c.sem, c.n)
            self._inced_refs.append(instr)

    def T(self, fn): self.blk.tensor(fn)
    def V(self, fn): self.blk.vector(fn)
    def SC(self, fn): self.blk.scalar(fn)
    def G(self, fn): self.blk.gpsimd(fn)
    def Y(self, fn): self.blk.sync(fn)


def ap3(t, off, nblk, blkstep, inner, freesz):
    return bass.AP(t, off, [[freesz, 128], [blkstep, nblk], [1, inner]])


def build_nc():
    nc = bass.Bass("TRN2", target_bir_lowering=False)
    I = {}
    for nm, shp, dt in [
        ("xkv", [T, DM], F32), ("xqo", [NOWN * 128, DM], F32),
        ("wqT", [DM, DM], F32), ("wkT", [DM, DM], F32),
        ("wvT", [DM, DM], F32), ("woT", [DM, DM], F32),
        ("w1", [DM, DFF], F32), ("w2", [DFF, DM], F32),
        ("thfm", [128, 32], F32), ("b1fm", [128, 32], F32),
        ("wrfm", [128, 32], F32), ("wifm", [128, 32], F32),
        ("b2bc", [128, DM], F32), ("btab", [128, NB + 1], F32),
        ("cbf", [128, 512], BF16),
    ]:
        I[nm] = nc.dram_tensor(nm, shp, dt, kind="ExternalInput").ap()
    yout = nc.dram_tensor("y", [NOWN * 128, DM], F32, kind="ExternalOutput").ap()
    if DBG_DUMP:
        z0 = {}
        z0["dbgk"] = nc.dram_tensor("dbgk", [8 * 128, T], BF16, kind="ExternalOutput").ap()
        z0["dbgv"] = nc.dram_tensor("dbgv", [NT * 128, DM], BF16, kind="ExternalOutput").ap()
        z0["dbgq"] = nc.dram_tensor("dbgq", [8 * 128, 512], BF16, kind="ExternalOutput").ap()
        z0["dbga"] = nc.dram_tensor("dbga", [8 * 128, 512], BF16, kind="ExternalOutput").ap()
        z0["dbgx1"] = nc.dram_tensor("dbgx1", [NOWN * 128, DM], F32, kind="ExternalOutput").ap()
        z0["dbgy2"] = nc.dram_tensor("dbgy2", [NOWN * 128, DM], F32, kind="ExternalOutput").ap()
    x1s = nc.dram_tensor("x1s", [NOWN * 128, DM], F32).ap()
    y2s = nc.dram_tensor("y2s", [NOWN * 128, DM], F32).ap()

    z = dict(I=I)
    if DBG_DUMP:
        z.update(z0)
    z["xkv_t"] = I["xkv"].rearrange("(t p) d -> t p d", p=128)
    z["xqo_t"] = I["xqo"].rearrange("(t p) d -> t p d", p=128)
    z["x1s_t"] = x1s.rearrange("(t p) d -> t p d", p=128)
    z["y2s_t"] = y2s.rearrange("(t p) d -> t p d", p=128)
    z["yout_t"] = yout.rearrange("(t p) d -> t p d", p=128)

    with ExitStack() as S:
        E = S.enter_context
        b = Build(nc, S)

        z["sc"] = [E(nc.psum_tensor(f"sc{i}", [128, 512], F32)) for i in range(3)]
        z["af"] = [E(nc.psum_tensor(f"af{i}", [128, 512], F32)) for i in range(2)]
        z["sm"] = [E(nc.psum_tensor(f"sm{i}", [128, 512], F32)) for i in range(2)]
        z["tpp"] = E(nc.psum_tensor("tpp", [128, 1024], BF16))

        z["cb"] = E(nc.sbuf_tensor("cb", [128, 512], BF16))
        z["bts"] = E(nc.sbuf_tensor("bts", [128, NB + 1], F32))
        z["xin"] = [E(nc.sbuf_tensor("xin0", [128, DM], F32))]
        z["sqs"] = E(nc.sbuf_tensor("sqs", [128, 512], F32))
        z["st4"] = E(nc.sbuf_tensor("st4", [128, 16], F32))
        z["mn1"] = E(nc.sbuf_tensor("mn1", [128, 32], F32))
        z["nv1"] = E(nc.sbuf_tensor("nv1", [128, 32], F32))
        z["rs1"] = E(nc.sbuf_tensor("rs1", [128, 32], F32))
        z["ng1"] = E(nc.sbuf_tensor("ng1", [128, 32], F32))
        z["r2o"] = E(nc.sbuf_tensor("r2o", [128, NOWN], F32))
        z["m2o"] = E(nc.sbuf_tensor("m2o", [128, NOWN], F32))
        z["nv3"] = E(nc.sbuf_tensor("nv3", [128, NOWN], F32))
        z["mn3"] = E(nc.sbuf_tensor("mn3", [128, NOWN], F32))
        z["r3o"] = E(nc.sbuf_tensor("r3o", [128, NOWN], F32))
        z["m3o"] = E(nc.sbuf_tensor("m3o", [128, NOWN], F32))
        z["lntm"] = E(nc.sbuf_tensor("lntm", [128, DM], BF16))
        z["ysb"] = E(nc.sbuf_tensor("ysb", [128, DM], F32))
        z["x1b"] = E(nc.sbuf_tensor("x1b", [128, DM], F32))

        with ExitStack() as SAo:
            EA = SAo.enter_context
            z["kfm"] = [EA(nc.sbuf_tensor(f"kfm{c}", [128, T], BF16)) for c in range(8)]
            z["vsb"] = [EA(nc.sbuf_tensor(f"vsb{t}", [128, 1040], BF16)) for t in range(NT)]

            with ExitStack() as SP:
                EP = SP.enter_context
                z["lnf"] = EP(nc.sbuf_tensor("lnf", [128, 8 * 512], BF16))
                z["wk"] = EP(nc.sbuf_tensor("wk", [128, 8 * DM], BF16))
                z["wv"] = EP(nc.sbuf_tensor("wv", [128, 8 * DM], BF16))
                with nc.Block() as blk:
                    b.blk = blk
                    emit_phaseP(b, z)
                    if DBG_DUMP:
                        def _dp(e):
                            b.w(e, "kv")
                            for c in range(8):
                                b.inc(e, e.dma_start(
                                    z["dbgk"].rearrange("(c p) t -> c p t", p=128)[c],
                                    z["kfm"][c][:]), "dmp", 16)
                            for t in range(DBG_NT):
                                b.inc(e, e.dma_start(
                                    z["dbgv"].rearrange("(t p) d -> t p d", p=128)[t],
                                    bass.AP(z["vsb"][t], 0,
                                            [[1040, 128], [65, 16], [1, 64]])), "dmp", 16)
                            dq1 = z["dbgx1"].rearrange("(t p) d -> t p d", p=128)[1]
                            dq = z["dbgx1"].rearrange("(t p) d -> t p d", p=128)[0]
                            b.inc(e, e.dma_start(dq[:, 0:32], z["mn1"][:]), "dmp", 16)
                            b.inc(e, e.dma_start(dq[:, 32:64], z["nv1"][:]), "dmp", 16)
                            b.inc(e, e.dma_start(dq[:, 64:96], z["rs1"][:]), "dmp", 16)
                            b.inc(e, e.dma_start(dq[:, 96:128], z["ng1"][:]), "dmp", 16)
                            b.w(e, "dmp")
                        b.Y(_dp)

                        def _dp2(e):
                            dq1 = z["dbgx1"].rearrange("(t p) d -> t p d", p=128)[1]
                            b.inc(e, e.dma_start(
                                dq1[:, 0:16],
                                bass.AP(z["vsb"][0], 64, [[1040, 128], [65, 16]]),
                                allow_slow_non_contiguous=True),
                                "dmp", 16)
                            b.w(e, "dmp")
                        b.G(_dp2)

                        def _dp3(e):
                            b.w(e, "dmp")
                        b.Y(_dp)

            with ExitStack() as SA:
                EA2 = SA.enter_context
                z["qbuf"] = [EA2(nc.sbuf_tensor(f"qbuf{c}", [128, 512], BF16))
                             for c in range(8)]
                z["lnqt"] = [EA2(nc.sbuf_tensor("lnqt0", [128, DM], BF16))]
                z["wq"] = EA2(nc.sbuf_tensor("wq", [128, 8 * DM], BF16))
                z["wo"] = EA2(nc.sbuf_tensor("wo", [128, 8 * DM], BF16))
                z["pb"] = [EA2(nc.sbuf_tensor(f"pb{i}", [128, 512], BF16))
                           for i in range(3)]
                z["rinv"] = EA2(nc.sbuf_tensor("rinv", [128, 512], F32))
                z["srow"] = EA2(nc.sbuf_tensor("srow", [1, 1024], BF16))
                z["lnt2"] = z["sqs"]
                z["afm"] = [EA2(nc.sbuf_tensor(f"afm{c}", [128, 512], BF16))
                            for c in range(8)]
                with nc.Block() as blk:
                    b.blk = blk
                    emit_phaseA(b, z)
                    if DBG_DUMP:
                        def _da(e):
                            b.w(e, "qrdy", "afmd", "x1w")
                            for c in range(8):
                                b.inc(e, e.dma_start(
                                    z["dbgq"].rearrange("(c p) t -> c p t", p=128)[c],
                                    z["qbuf"][c][:]), "dmp", 16)
                                b.inc(e, e.dma_start(
                                    z["dbga"].rearrange("(c p) t -> c p t", p=128)[c],
                                    z["afm"][c][:]), "dmp", 16)
                            for oi in range(4 * DBG_NQG):
                                b.inc(e, e.dma_start(
                                    z["dbgx1"].rearrange("(t p) d -> t p d", p=128)[oi],
                                    z["x1s_t"][oi]), "dmp", 16)
                            b.w(e, "dmp")
                        b.Y(_da)

        with ExitStack() as SF:
            EF = SF.enter_context
            z["w1s"] = [EF(nc.sbuf_tensor(f"w1s{d}", [128, DFF], BF16)) for d in range(8)]
            z["w2s"] = EF(nc.sbuf_tensor("w2s", [128, 32 * DM], BF16))
            z["b2s"] = EF(nc.sbuf_tensor("b2s", [128, DM], F32))
            z["ths"] = EF(nc.sbuf_tensor("ths", [128, 32], F32))
            z["b1s"] = EF(nc.sbuf_tensor("b1s", [128, 32], F32))
            z["wrs"] = EF(nc.sbuf_tensor("wrs", [128, 32], F32))
            z["wis"] = EF(nc.sbuf_tensor("wis", [128, 32], F32))
            z["cfs"] = EF(nc.sbuf_tensor("cfs", [128, 32], F32))
            z["bth"] = EF(nc.sbuf_tensor("bth", [128, 32], F32))
            z["ln2f"] = EF(nc.sbuf_tensor("ln2f", [128, 8 * 256], BF16))
            z["hsb"] = [EF(nc.sbuf_tensor(f"hsb{i}", [128, 256], F32)) for i in range(2)]
            z["asb"] = [EF(nc.sbuf_tensor(f"asb{i}", [128, 256], F32)) for i in range(2)]
            z["ksb"] = [EF(nc.sbuf_tensor(f"ksb{i}", [128, 256], I32)) for i in range(2)]
            z["rsb"] = [EF(nc.sbuf_tensor(f"rsb{i}", [128, 256], F32)) for i in range(2)]
            z["qvb"] = [EF(nc.sbuf_tensor(f"qvb{i}", [128, 256], F32)) for i in range(2)]
            z["acb"] = [EF(nc.sbuf_tensor(f"acb{i}", [128, 256], BF16)) for i in range(3)]
            z["y2b"] = [EF(nc.sbuf_tensor(f"y2b{i}", [128, DM], F32)) for i in range(2)]
            z["osb"] = [EF(nc.sbuf_tensor(f"osb{i}", [128, DM], F32)) for i in range(2)]
            with nc.Block() as blk:
                b.blk = blk
                if DBG_NQG == NQG:
                    emit_phaseF(b, z)
                    if DBG_DUMP:
                        def _df(e):
                            b.w(e, "y2w")
                            for oi in range(2 * DBG_NGG):
                                b.inc(e, e.dma_start(
                                    z["dbgy2"].rearrange("(t p) d -> t p d", p=128)[oi],
                                    z["y2s_t"][oi]), "dmp", 16)
                            b.w(e, "dmp")
                        b.Y(_df)
                    if DBG_FIN and DBG_NGG == 8:
                        emit_phaseFIN(b, z)

    return nc


def ln_stats(b, z, src, pi, done_ch, wait=(), mean_dst=None, nv_dst=None,
             rstd=True, free_ch=None):
    st4 = z["st4"]
    cs, cq = st4[:, pi:pi + 1], st4[:, 2 + pi:3 + pi]
    cm = mean_dst if mean_dst is not None else st4[:, 4 + pi:5 + pi]
    cn = nv_dst if nv_dst is not None else st4[:, 6 + pi:7 + pi]
    cr, cg = st4[:, 8 + pi:9 + pi], st4[:, 10 + pi:11 + pi]
    ct = st4[:, 12 + pi:13 + pi]
    c2 = st4[:, 14 + pi:15 + pi]

    def _(e):
        b.w(e, *wait)
        e.reduce_sum(cs, src, axis=AX.X)
        i0 = e.scalar_tensor_tensor(z["sqs"][:], src[:, 0:512], 1.0, src[:, 0:512],
                               ALU.mult, ALU.mult, accum_out=cq)
        if SIMSAFE:
            b.inc(e, i0, "sqacc")
            b.w(e, "sqacc")
        i1 = e.scalar_tensor_tensor(z["sqs"][:], src[:, 512:1024], 1.0,
                                    src[:, 512:1024], ALU.mult, ALU.mult,
                                    accum_out=c2)
        e.drain()
        e.tensor_tensor(cq, cq, c2, ALU.add)
        e.tensor_scalar(cm, cs, 1.0 / DM, None, ALU.mult)
        e.tensor_scalar(ct, cq, 1.0 / DM, None, ALU.mult)
        i = e.scalar_tensor_tensor(cn, cm, cm, ct, ALU.mult, ALU.subtract)
        b.inc(e, i, "stD")
        if not rstd:
            b.inc(e, i, done_ch)
        if free_ch:
            b.inc(e, i, free_ch)
    b.V(_)
    if not rstd:
        return

    def _(e):
        b.w(e, "stD", "w4")
        i1 = e.activation(ct, cn, AF.Ln, bias=z["bts"][:, NB:NB + 1], scale=-1.0)
        b.inc(e, i1, "actf")
        b.w(e, "actf")
        b.inc(e, e.activation(cr, ct, AF.Exp, scale=-0.5), "stA")
    b.SC(_)

    def _(e):
        b.w(e, "stA")
        b.inc(e, e.tensor_scalar(cg, cm, cr, -1.0, ALU.mult, ALU.mult), done_ch)
    b.V(_)


def emit_phaseP(b, z):
    NTP = DBG_NT
    I = z["I"]
    st4, tpp, lnf = z["st4"], z["tpp"], z["lnf"]
    vb = [z["sc"][0], z["sc"][1], z["af"][0], z["af"][1]]
    kb = [z["sm"][0], z["sm"][1]]

    def _(e):
        for src, dst in [(I["cbf"], z["cb"]), (I["btab"], z["bts"])]:
            b.inc(e, e.dma_start(dst[:], src), "w4", 16)
    b.Y(_)

    def _(e):
        for w_h, w_s in [(I["wkT"], z["wk"]), (I["wvT"], z["wv"])]:
            b.inc(e, e.dma_start(w_s[:].rearrange("p (c o) -> p c o", c=8),
                              w_h.rearrange("(c p) o -> p c o", p=128)),
                  "wkv", 16)
        for t in range(NT):
            i = e.memset(z["vsb"][t][:, 64:1040:65], 1.0)
        b.inc(e, i, "qz")
    b.G(_)

    for t in range(NTP):
        pi = t % 2

        def _(e, t=t, pi=pi):
            b.w(e, "xfree")
            b.inc(e, e.dma_start(z["xin"][0][:], z["xkv_t"][t]), "xin", 16)
        b.Y(_)

        ln_stats(b, z, z["xin"][0][:], pi, "ln1", wait=("xin",),
                 mean_dst=z["mn1"][:, t:t + 1], nv_dst=z["nv1"][:, t:t + 1],
                 rstd=False, free_ch="xfree")

    def _(e):
        b.w(e, "ln1", "w4")
        i = e.activation(z["ng1"][:, 0:NTP], z["nv1"][:, 0:NTP], AF.Ln,
                         bias=z["bts"][:, NB:NB + 1], scale=-1.0)
        b.inc(e, i, "actf")
        b.w(e, "actf")
        i = e.activation(z["rs1"][:, 0:NTP], z["ng1"][:, 0:NTP], AF.Exp,
                         scale=-0.5)
        b.inc(e, i, "r1d")
    b.SC(_)

    def _(e):
        b.w(e, "r1d")
        i = e.scalar_tensor_tensor(z["ng1"][:, 0:NTP], z["mn1"][:, 0:NTP],
                                   -1.0, z["rs1"][:, 0:NTP],
                                   ALU.mult, ALU.mult)
        b.inc(e, i, "g1d")
    b.V(_)

    for t in range(NTP):
        pi = t % 2

        def _(e, t=t, pi=pi):
            b.w(e, "xfree")
            b.inc(e, e.dma_start(z["xin"][0][:], z["xkv_t"][t]), "xin", 16)
        b.Y(_)

        def _(e, t=t, pi=pi):
            b.w(e, "xin", "g1d", "tp")
            i = e.tensor_scalar(z["lntm"][:], z["xin"][0][:],
                                z["rs1"][:, t:t + 1], z["ng1"][:, t:t + 1],
                                ALU.mult, ALU.add)
            b.inc(e, i, "lnrdy")
            b.inc(e, i, "xfree")
        b.V(_)

        def _(e, pi=pi):
            b.w(e, "lnrdy", "tpe", "w4", "wkv")
            for c in range(8):
                i = e.transpose(tpp[0:128, c * 128:(c + 1) * 128],
                                z["lntm"][:, c * 128:(c + 1) * 128],
                                z["cb"][:, 0:128])
            b.inc(e, i, "tp")
        b.T(_)

        def _(e, t=t):
            b.w(e, "tp")
            if t % 4 == 0:
                b.w(e, "lnffree")
            i = e.tensor_copy(ap3(lnf, (t % 4) * 128, 8, 512, 128, 4096),
                              ap3(tpp, 0, 8, 128, 128, 1024))
            b.inc(e, i, "lnfd")
            b.inc(e, i, "tpe")
        b.V(_)

        for half in range(2):
            bank = vb[2 * (t % 2) + half]
            bch, fch = f"vpd{2*(t%2)+half}", f"vpf{2*(t%2)+half}"

            def _(e, t=t, half=half, bank=bank, bch=bch, fch=fch):
                b.w(e, "lnfd", fch)
                for d in range(8):
                    i = e.matmul(bank[0:128, 0:512],
                                 z["lnf"][:, d * 512 + (t % 4) * 128:
                                          d * 512 + (t % 4) * 128 + 128],
                                 z["wv"][:, d * 1024 + half * 512:
                                         d * 1024 + half * 512 + 512],
                                 start=(d == 0), stop=(d == 7))
                b.inc(e, i, bch)
            b.T(_)

            def _(e, t=t, half=half, bank=bank, bch=bch, fch=fch):
                b.w(e, bch)
                i = e.tensor_copy(
                    bass.AP(z["vsb"][t], half * 520, [[1040, 128], [65, 8], [1, 64]]),
                    bass.AP(bank, 0, [[512, 128], [64, 8], [1, 64]]))
                b.inc(e, i, "kv")
                b.inc(e, i, fch)
            b.V(_)

        if t % 4 == 3:
            g = t // 4
            for c in range(DBG_KC):
                bank = kb[c % 2]
                bch, fch = f"kpd{c%2}", f"kpf{c%2}"

                def _(e, c=c, bank=bank, bch=bch, fch=fch):
                    b.w(e, fch)
                    for d in range(8):
                        i = e.matmul(bank[0:128, 0:512],
                                     z["wk"][:, d * 1024 + c * 128:
                                             d * 1024 + c * 128 + 128],
                                     z["lnf"][:, d * 512:d * 512 + 512],
                                     start=(d == 0), stop=(d == 7))
                    b.inc(e, i, bch)
                b.T(_)

                def _(e, c=c, g=g, bank=bank, bch=bch, fch=fch):
                    b.w(e, bch)
                    if DBG_KEV:
                        i = e.tensor_copy(z["kfm"][c][:, g * 512:g * 512 + 512],
                                          bank[0:128, 0:512])
                    else:
                        i = e.nop()
                    b.inc(e, i, "kv")
                    b.inc(e, i, fch)
                    if c == 7:
                        b.inc(e, i, "lnffree")
                b.V(_)


def emit_phaseA(b, z):
    I = z["I"]
    st4, tpp = z["st4"], z["tpp"]
    scr = z["sc"]
    rp = [0]
    pp = [0]

    def _(e):
        for w_h, w_s in [(I["wqT"], z["wq"]), (I["woT"], z["wo"])]:
            b.inc(e, e.dma_start(w_s[:].rearrange("p (c o) -> p c o", c=8),
                              w_h.rearrange("(c p) o -> p c o", p=128)),
                  "wA", 16)
    b.G(_)

    for a in range(DBG_NQG):
        for jj in range(4):
            oi = 4 * a + jj
            pi = jj % 2

            def _(e, oi=oi, pi=pi):
                b.w(e, "xfree")
                b.inc(e, e.dma_start(z["xin"][0][:], z["xqo_t"][oi]), "xin", 16)
            b.Y(_)

            ln_stats(b, z, z["xin"][0][:], pi, "lnA", wait=("xin", "rif", "g1d"),
                     mean_dst=z["mn1"][:, jj:jj + 1],
                     nv_dst=z["nv1"][:, jj:jj + 1], rstd=False,
                     free_ch="xfree")

        def _(e):
            b.w(e, "lnA", "w4")
            i = e.activation(z["ng1"][:, 0:4], z["nv1"][:, 0:4], AF.Ln,
                             bias=z["bts"][:, NB:NB + 1], scale=-1.0)
            b.inc(e, i, "actf")
            b.w(e, "actf")
            i = e.activation(z["rs1"][:, 0:4], z["ng1"][:, 0:4], AF.Exp,
                             scale=-0.5)
            b.inc(e, i, "r1d")
        b.SC(_)

        def _(e):
            b.w(e, "r1d")
            i = e.scalar_tensor_tensor(z["ng1"][:, 0:4], z["mn1"][:, 0:4],
                                       -1.0, z["rs1"][:, 0:4],
                                       ALU.mult, ALU.mult)
            b.inc(e, i, "g1d")
        b.V(_)

        for jj in range(4):
            oi = 4 * a + jj
            pi = jj % 2

            def _(e, oi=oi, pi=pi):
                b.w(e, "xfree")
                b.inc(e, e.dma_start(z["xin"][0][:], z["xqo_t"][oi]), "xin", 16)
            b.Y(_)

            def _(e, jj=jj, pi=pi):
                b.w(e, "xin", "g1d", "tp", "rif")
                i = e.tensor_scalar(z["lntm"][:], z["xin"][0][:],
                                    z["rs1"][:, jj:jj + 1],
                                    z["ng1"][:, jj:jj + 1],
                                    ALU.mult, ALU.add)
                b.inc(e, i, "lnrdyA")
                b.inc(e, i, "xfree")
            b.V(_)

            def _(e, pi=pi):
                b.w(e, "lnrdyA", "tpe")
                for c in range(8):
                    i = e.transpose(tpp[0:128, c * 128:(c + 1) * 128],
                                    z["lntm"][:, c * 128:(c + 1) * 128],
                                    z["cb"][:, 0:128])
                b.inc(e, i, "tp")
            b.T(_)

            def _(e, pi=pi):
                b.w(e, "tp", "lqf")
                i = e.tensor_copy(z["lnqt"][0][:], tpp[0:128, 0:1024])
                b.inc(e, i, "lqd")
                b.inc(e, i, "tpe")
            b.V(_)

            for c in range(8):
                bank = z["af"][c % 2]
                bch, fch = f"qpd{c%2}", f"qpf{c%2}"

                def _(e, c=c, pi=pi, bank=bank, bch=bch, fch=fch):
                    b.w(e, "lqd", fch, "wA", f"aff{c%2}")
                    for d in range(8):
                        i = e.matmul(bank[0:128, 0:128],
                                     z["wq"][:, d * 1024 + c * 128:
                                             d * 1024 + c * 128 + 128],
                                     z["lnqt"][0][:, d * 128:d * 128 + 128],
                                     start=(d == 0), stop=(d == 7))
                    b.inc(e, i, bch)
                b.T(_)

                def _(e, c=c, pi=pi, jj=jj, bank=bank, bch=bch, fch=fch):
                    b.w(e, bch, "qfree")
                    i = e.tensor_scalar(
                        z["qbuf"][c][:, jj * 128:jj * 128 + 128],
                        bank[0:128, 0:128], 0.125, None, ALU.mult)
                    b.inc(e, i, "qrdy")
                    b.inc(e, i, fch)
                    if c == 7:
                        b.inc(e, i, "lqf")
                b.V(_)

        ring = [z["sc"][0], z["sc"][1], z["sc"][2], z["sm"][1]]
        pend = []
        for c in range(DBG_PAIRS if DBG_A >= 2 else 0):
            h0, h1 = 2 * c, 2 * c + 1
            lo0, lo1 = band_lo(h0, a), band_lo(h1, a)
            slots = list(range(min(lo0, lo1), 8 * a + 8))
            nsl = len(slots)

            def sc_sec(si, c=c, lo0=lo0, slots=slots):
                ktg = slots[si]
                jjs, qs = slot_geom(a, ktg)
                in0 = ktg >= lo0
                rbA, rbB = (2 * si) % 4, (2 * si + 1) % 4

                def _(e, c=c, ktg=ktg, qs=qs, rbA=rbA, rbB=rbB, in0=in0):
                    if in0:
                        b.w(e, f"scf{rbA}")
                    b.w(e, f"scf{rbB}", "qrdy", "kv", "qz")
                    if in0:
                        i = e.matmul(ring[rbA][0:128, qs:512],
                                     z["kfm"][c][0:64, ktg * 128:ktg * 128 + 128],
                                     z["qbuf"][c][0:64, qs:512],
                                     start=True, stop=True, tile_position=(0, 0))
                        b.inc(e, i, f"scd{rbA}")
                    i = e.matmul(ring[rbB][0:128, qs:512],
                                 z["kfm"][c][64:128, ktg * 128:ktg * 128 + 128],
                                 z["qbuf"][c][64:128, qs:512],
                                 start=True, stop=True, tile_position=(64, 0))
                    b.inc(e, i, f"scd{rbB}")
                b.T(_)

            def exp_sec(si, c=c, h0=h0, h1=h1, lo0=lo0, slots=slots):
                ktg = slots[si]
                jjs, qs = slot_geom(a, ktg)
                d = ktg - 8 * a
                in0 = ktg >= lo0
                rbA, rbB = (2 * si) % 4, (2 * si + 1) % 4
                pA, pB = (2 * si) % 3, (2 * si + 1) % 3

                def _(e, h0=h0, h1=h1, d=d, qs=qs, rbA=rbA, rbB=rbB,
                      pA=pA, pB=pB, in0=in0):
                    if in0:
                        b.w(e, f"scd{rbA}", f"pbf{pA}")
                    b.w(e, f"scd{rbB}", f"pbf{pB}")
                    if in0:
                        for (cc, c0, c1) in exp_chunks(h0, qs):
                            col = BCOLS[(h0, d, cc)]
                            i = e.activation(z["pb"][pA][:, c0:c1],
                                             ring[rbA][0:128, c0:c1], AF.Exp,
                                             bias=z["bts"][:, col:col + 1])
                        b.inc(e, i, f"pbd{pA}")
                        b.inc(e, i, f"scf{rbA}")
                    for (cc, c0, c1) in exp_chunks(h1, qs):
                        col = BCOLS[(h1, d, cc)]
                        i = e.activation(z["pb"][pB][:, c0:c1],
                                         ring[rbB][0:128, c0:c1], AF.Exp,
                                         bias=z["bts"][:, col:col + 1])
                    b.inc(e, i, f"pbd{pB}")
                    b.inc(e, i, f"scf{rbB}")
                b.SC(_)

            def msk_sec(si, slots=slots, lo0=lo0):
                ktg = slots[si]
                jjs, qs = slot_geom(a, ktg)
                d = ktg - 8 * a
                if d < 0:
                    return f"pbd{(2*si)%3}", f"pbd{(2*si+1)%3}"
                in0 = ktg >= lo0
                pA, pB = (2 * si) % 3, (2 * si + 1) % 3
                mcol = 128 if d % 2 == 0 else 256

                def _(e, qs=qs, pA=pA, pB=pB, mcol=mcol, in0=in0):
                    if in0:
                        b.w(e, f"pbd{pA}")
                        i = e.tensor_mul(z["pb"][pA][:, qs:qs + 128],
                                         z["pb"][pA][:, qs:qs + 128],
                                         z["cb"][:, mcol:mcol + 128])
                        b.inc(e, i, f"pbm{pA}")
                    b.w(e, f"pbd{pB}")
                    i = e.tensor_mul(z["pb"][pB][:, qs:qs + 128],
                                     z["pb"][pB][:, qs:qs + 128],
                                     z["cb"][:, mcol:mcol + 128])
                    b.inc(e, i, f"pbm{pB}")
                b.G(_)
                return f"pbm{pA}", f"pbm{pB}"

            def av_sec(si, mchA, mchB, c=c, lo0=lo0, lo1=lo1, slots=slots):
                ktg = slots[si]
                jjs, qs = slot_geom(a, ktg)
                in0 = ktg >= lo0
                pA, pB = (2 * si) % 3, (2 * si + 1) % 3
                first0, first1 = (ktg == lo0), (ktg == lo1)
                last = (ktg == 8 * a + 7)

                def _(e, c=c, ktg=ktg, qs=qs, pA=pA, pB=pB, in0=in0,
                      first0=first0, first1=first1, last=last,
                      mchA=mchA, mchB=mchB):
                    if in0:
                        b.w(e, mchA)
                        if first0:
                            b.w(e, "aff0", "qpf0")
                        i = e.matmul(z["af"][0][0:65, qs:512],
                                     z["vsb"][ktg][:, 65 * 2 * c:65 * 2 * c + 65],
                                     z["pb"][pA][:, qs:512],
                                     start=first0, stop=last)
                        b.inc(e, i, f"pbf{pA}")
                        if last:
                            b.inc(e, i, "avd0")
                    b.w(e, mchB)
                    if first1:
                        b.w(e, "aff1", "qpf1")
                    i = e.matmul(z["af"][1][0:65, qs:512],
                                 z["vsb"][ktg][:, 65 * (2 * c + 1):65 * (2 * c + 1) + 65],
                                 z["pb"][pB][:, qs:512],
                                 start=first1, stop=last)
                    b.inc(e, i, f"pbf{pB}")
                    if last:
                        b.inc(e, i, "avd1")
                b.T(_)

            def tail(c=c):
                def _(e, c=c):
                    b.w(e, "avd0", "avd1", "bcf")
                    e.tensor_copy(z["srow"][0:1, 0:512],
                                  z["af"][0][64:65, 0:512])
                    i = e.tensor_copy(z["srow"][0:1, 512:1024],
                                      z["af"][1][64:65, 0:512])
                    b.inc(e, i, "s64d")
                b.V(_)

                def _(e, c=c):
                    b.w(e, "s64d", "smf0")
                    e.matmul(z["sm"][0][0:64, 0:512], z["cb"][0:1, 384:448],
                             z["srow"][0:1, 0:512], start=True, stop=True,
                             tile_position=(0, 0), skip_group_check=True)
                    i = e.matmul(z["sm"][0][64:128, 0:512],
                                 z["cb"][0:1, 384:448],
                                 z["srow"][0:1, 512:1024], start=True,
                                 stop=True, tile_position=(0, 64),
                                 skip_group_check=True)
                    b.inc(e, i, "bcd")
                    b.inc(e, i, "bcf")
                b.T(_)

                def _(e, c=c):
                    b.w(e, "bcd", "rif")
                    e.activation(z["lnt2"][0:64, 0:512],
                                 z["sm"][0][0:64, 0:512], AF.Ln)
                    i1 = e.activation(z["lnt2"][64:128, 0:512],
                                      z["sm"][0][64:128, 0:512], AF.Ln)
                    b.inc(e, i1, "actf")
                    b.w(e, "actf")
                    i = e.activation(z["rinv"][:, 0:512], z["lnt2"][:, 0:512],
                                     AF.Exp, scale=-1.0)
                    b.inc(e, i, "rin")
                    b.inc(e, i, "smf0")
                b.SC(_)

                def _(e, c=c):
                    b.w(e, "rin", "afmf")
                    e.tensor_tensor(z["afm"][c][0:64, :],
                                    z["af"][0][0:64, 0:512],
                                    z["rinv"][0:64, 0:512], ALU.mult)
                    i = e.tensor_tensor(z["afm"][c][64:128, :],
                                        z["af"][1][0:64, 0:512],
                                        z["rinv"][64:128, 0:512], ALU.mult)
                    b.inc(e, i, "afmd")
                    b.inc(e, i, "aff0")
                    b.inc(e, i, "aff1")
                    b.inc(e, i, "rif")
                    if c == DBG_PAIRS - 1:
                        b.inc(e, i, "qfree")
                b.V(_)

            sc_sec(0)
            if nsl > 1:
                sc_sec(1)
            if pend:
                pend.pop()()
            for si in range(nsl):
                mchA, mchB = None, None
                exp_sec(si)
                mchA, mchB = msk_sec(si)
                if si + 2 < nsl:
                    sc_sec(si + 2)
                av_sec(si, mchA, mchB)
            pend.append(tail)

        if pend:
            pend.pop()()



        for jj in range(4 if (DBG_A >= 3 and DBG_PAIRS == 8) else 0):
            oi = 4 * a + jj
            pi = jj % 2
            banks = [z["sm"][0], z["sm"][1]] if pi == 0 else [z["af"][0], z["af"][1]]
            bch = f"wod{pi}"
            fch0 = ("smf0", "scf3") if pi == 0 else ("aff0", "aff1")

            if DBG_A4 >= 1:
                def _(e, oi=oi, pi=pi):
                    b.w(e, "xfree")
                    b.inc(e, e.dma_start(z["xin"][0][:], z["xqo_t"][oi]), "xr", 16)
                b.Y(_)
            if DBG_A4 < 2:
                continue

            def _(e, jj=jj, banks=banks, bch=bch, fch0=fch0):
                b.w(e, "afmd", *fch0)
                for half in range(2):
                    for cc in range(8):
                        i = e.matmul(banks[half][0:128, 0:512],
                                     z["afm"][cc][:, jj * 128:jj * 128 + 128],
                                     z["wo"][:, cc * 1024 + half * 512:
                                             cc * 1024 + half * 512 + 512],
                                     start=(cc == 0), stop=(cc == 7))
                b.inc(e, i, bch)
            b.T(_)
            if DBG_A4 < 3:
                continue

            if DBG_YEV == 0:
                def _(e, jj=jj, banks=banks, bch=bch, fch0=fch0):
                    b.w(e, bch, "yfree")
                    e.activation(z["ysb"][:, 0:512], banks[0][0:128, 0:512],
                                 AF.Identity)
                    i = e.activation(z["ysb"][:, 512:1024],
                                     banks[1][0:128, 0:512], AF.Identity)
                    b.inc(e, i, "yev")
                    b.inc(e, i, fch0[0])
                    b.inc(e, i, fch0[1])
                    if jj == 3:
                        b.inc(e, i, "afmf")
                b.SC(_)
            else:
                def _(e, jj=jj, banks=banks, bch=bch, fch0=fch0):
                    b.w(e, bch, "yfree")
                    e.tensor_copy(z["ysb"][:, 0:512], banks[0][0:128, 0:512])
                    i = e.tensor_copy(z["ysb"][:, 512:1024],
                                      banks[1][0:128, 0:512])
                    b.inc(e, i, "yev")
                    b.inc(e, i, fch0[0])
                    b.inc(e, i, fch0[1])
                    if jj == 3:
                        b.inc(e, i, "afmf")
                b.V(_)
            if DBG_A4 < 4:
                continue

            if DBG_LNY == 0:
                ln_stats(b, z, z["ysb"][:], pi, "lnY", wait=("yev",))
            elif DBG_LNY == 1:
                ln_stats(b, z, z["ysb"][:], pi, "lnY", wait=("yev",), rstd=False)
            elif DBG_LNY == 2:
                ln_stats(b, z, z["xin"][0][:], pi, "lnY", wait=("yev",))
            elif DBG_LNY == 3:
                ln_stats(b, z, z["ysb"][:], pi, "lnYx", wait=("yev",), rstd=False)
                def _f3(e, pi=pi):
                    b.w(e, "lnYx")
                    e.activation(z["st4"][:, 12 + pi:13 + pi],
                                 z["st4"][:, 6 + pi:7 + pi], AF.Ln,
                                 bias=z["bts"][:, NB:NB + 1], scale=-1.0)
                    e.drain()
                    i = e.activation(z["st4"][:, 8 + pi:9 + pi],
                                     z["st4"][:, 12 + pi:13 + pi], AF.Exp,
                                     scale=-0.5)
                    b.inc(e, i, "lnY")
                b.SC(_f3)
            elif DBG_LNY == 4:
                ln_stats(b, z, z["ysb"][:], pi, "lnYx", wait=("yev",), rstd=False)
                def _f4(e, pi=pi):
                    b.w(e, "lnYx")
                    i = e.tensor_scalar(z["st4"][:, 10 + pi:11 + pi],
                                        z["st4"][:, 4 + pi:5 + pi],
                                        z["st4"][:, 8 + pi:9 + pi], -1.0,
                                        ALU.mult, ALU.mult)
                    b.inc(e, i, "lnY")
                b.V(_f4)
            if DBG_A4 < 5:
                continue

            def _(e, pi=pi):
                b.w(e, "lnY")
                i = e.tensor_scalar(z["ysb"][:], z["ysb"][:],
                                    st4[:, 8 + pi:9 + pi],
                                    st4[:, 10 + pi:11 + pi],
                                    ALU.mult, ALU.add)
                b.inc(e, i, "ylnd")
            b.V(_)
            if DBG_A4 < 6:
                continue

            def _(e, pi=pi):
                b.w(e, "ylnd", "xr", "x1w", "ln2S")
                i = e.tensor_add(z["x1b"][:], z["ysb"][:], z["xin"][0][:])
                b.inc(e, i, "x1d")
            b.G(_)

            def _(e):
                b.w(e, "x1d")
                b.inc(e, e.nop(), "xfree")
                b.inc(e, e.nop(), "yfree")
            b.V(_)
            if DBG_A4 < 7:
                continue

            ln_stats(b, z, z["x1b"][:], pi, "ln2S", wait=("x1d",))
            if DBG_A4 < 8:
                continue

            def _(e, oi=oi, pi=pi):
                b.w(e, "ln2S")
                e.tensor_copy(z["r2o"][:, oi:oi + 1], st4[:, 8 + pi:9 + pi])
                i = e.tensor_copy(z["m2o"][:, oi:oi + 1],
                                  st4[:, 10 + pi:11 + pi])
                b.inc(e, i, "r2d")
            b.V(_)
            if DBG_A4 < 9:
                continue

            def _(e, oi=oi):
                b.w(e, "r2d")
                b.inc(e, e.dma_start(z["x1s_t"][oi], z["x1b"][:]), "x1w", 16)
            b.Y(_)


def emit_phaseF(b, z):
    I = z["I"]
    st4, tpp = z["st4"], z["tpp"]
    hid = [z["sc"][0], z["sc"][1], z["sc"][2]]
    y2bank = [[z["af"][0], z["af"][1]], [z["sm"][0], z["sm"][1]]]
    NF = DFF // 128

    def _(e):
        for d in range(8):
            src = I["w1"].rearrange("(c p) f -> c p f", p=128)[d]
            b.inc(e, e.dma_start(z["w1s"][d][:], src), "w12g", 16)
        b.inc(e, e.dma_start(z["w2s"][:].rearrange("p (c o) -> p c o", c=32),
                          I["w2"].rearrange("(c p) o -> p c o", p=128)),
              "w12g", 16)
    b.G(_)

    def _(e):
        b.inc(e, e.dma_start(z["b2s"][:], I["b2bc"]), "xr", 16)
        for src, dst in [(I["thfm"], z["ths"]), (I["b1fm"], z["b1s"]),
                         (I["wrfm"], z["wrs"]), (I["wifm"], z["wis"])]:
            b.inc(e, e.dma_start(dst[:], src), "xr", 16)
    b.Y(_)

    def _(e):
        b.w(e, "xr")
        k32 = z["ksb"][0][:, 0:32]
        e.tensor_scalar(k32, z["ths"][:], 1.0 / TWO_PI, None, ALU.mult)
        e.scalar_tensor_tensor(z["rsb"][0][:, 0:32], k32, -TWO_PI, z["ths"][:],
                               ALU.mult, ALU.add)
        e.tensor_scalar(z["rsb"][0][:, 32:64], z["rsb"][0][:, 0:32],
                        PI_CLAMP, -PI_CLAMP, ALU.min, ALU.max)
        e.tensor_scalar(z["asb"][0][:, 0:32], z["ths"][:], 1.0,
                        float(np.pi / 2), ALU.mult, ALU.add)
        e.tensor_scalar(k32, z["asb"][0][:, 0:32], 1.0 / TWO_PI, None, ALU.mult)
        e.scalar_tensor_tensor(z["rsb"][0][:, 64:96], k32, -TWO_PI,
                               z["asb"][0][:, 0:32], ALU.mult, ALU.add)
        i = e.tensor_scalar(z["rsb"][0][:, 96:128], z["rsb"][0][:, 64:96],
                            PI_CLAMP, -PI_CLAMP, ALU.min, ALU.max)
        b.inc(e, i, "cfD")
    b.V(_)

    def _(e):
        b.w(e, "cfD")
        e.activation(z["asb"][1][:, 0:32], z["rsb"][0][:, 32:64], AF.Sin)
        i = e.activation(z["asb"][1][:, 32:64], z["rsb"][0][:, 96:128], AF.Sin)
        b.inc(e, i, "cfA")
    b.SC(_)

    def _(e):
        b.w(e, "cfA")
        e.tensor_tensor(z["qvb"][0][:, 0:32], z["asb"][1][:, 32:64],
                        z["wrs"][:], ALU.mult)
        e.tensor_tensor(z["qvb"][0][:, 32:64], z["asb"][1][:, 0:32],
                        z["wis"][:], ALU.mult)
        e.tensor_tensor(z["qvb"][0][:, 64:96], z["qvb"][0][:, 0:32],
                        z["qvb"][0][:, 32:64], ALU.add)
        e.tensor_tensor(z["qvb"][0][:, 96:128], z["qvb"][0][:, 64:96],
                        z["wis"][:], ALU.mult)
        e.tensor_scalar(z["cfs"][:], z["qvb"][0][:, 96:128], 0.1, None,
                        ALU.mult)
        i = e.scalar_tensor_tensor(z["bth"][:], z["b1s"][:], 0.1, z["ths"][:],
                                   ALU.mult, ALU.add)
        b.inc(e, i, "cfd")
    b.V(_)

    for gg in range(DBG_NGG):
        for k in range(2):
            oi = 2 * gg + k

            def _(e, oi=oi, k=k):
                b.w(e, "ffxf")
                b.inc(e, e.dma_start(z["xin"][0][:], z["x1s_t"][oi]), "ffx", 16)
            b.Y(_)

            def _(e, oi=oi, k=k):
                b.w(e, "ffx", "tp")
                i = e.tensor_scalar(z["lntm"][:], z["xin"][0][:],
                                    z["r2o"][:, oi:oi + 1],
                                    z["m2o"][:, oi:oi + 1], ALU.mult, ALU.add)
                b.inc(e, i, "ln2d")
                b.inc(e, i, "ffxf")
            b.V(_)

            def _(e, k=k):
                b.w(e, "ln2d", "tpe")
                for c in range(8):
                    i = e.transpose(tpp[0:128, c * 128:(c + 1) * 128],
                                    z["lntm"][:, c * 128:(c + 1) * 128],
                                    z["cb"][:, 0:128])
                b.inc(e, i, "tp")
            b.T(_)

            def _(e, k=k):
                b.w(e, "tp")
                if k == 0:
                    b.w(e, "ln2free")
                i = e.tensor_copy(ap3(z["ln2f"], k * 128, 8, 256, 128, 2048),
                                  ap3(tpp, 0, 8, 128, 128, 1024))
                b.inc(e, i, "ln2fd")
                b.inc(e, i, "tpe")
            b.V(_)

        def hid_sec(f):
            hb = f % 3

            def _(e, f=f, hb=hb):
                b.w(e, "ln2fd", f"hidf{hb}", "w12g", "xr", "cfd")
                for dd in range(8):
                    i = e.matmul(hid[hb][0:128, 0:256],
                                 z["w1s"][dd][:, f * 128:f * 128 + 128],
                                 z["ln2f"][:, dd * 256:dd * 256 + 256],
                                 start=(dd == 0), stop=(dd == 7))
                if f == NF - 1:
                    b.inc(e, i, "ln2free")
                b.inc(e, i, f"hidd{hb}")
            b.T(_)

        hid_sec(0)
        hid_sec(1)
        hid_sec(2)
        for f in range(NF):
            hb, sp, ab = f % 3, f % 2, f % 3

            def _(e, f=f, hb=hb, sp=sp):
                b.w(e, f"hidd{hb}", f"sinA{sp}")
                e.tensor_scalar(z["asb"][sp][:], hid[hb][0:128, 0:256], 0.1,
                                z["bth"][:, f:f + 1], ALU.mult, ALU.add)
                e.tensor_scalar(z["ksb"][sp][:], z["asb"][sp][:],
                                1.0 / TWO_PI, None, ALU.mult)
                i = e.scalar_tensor_tensor(z["rsb"][sp][:], z["ksb"][sp][:],
                                           -TWO_PI, z["asb"][sp][:],
                                           ALU.mult, ALU.add)
                b.inc(e, i, f"kD{sp}")
            b.V(_)

            def _(e, sp=sp):
                b.w(e, f"kD{sp}")
                i = e.activation(z["asb"][sp][:], z["rsb"][sp][:], AF.Sin)
                b.inc(e, i, f"sinA{sp}")
            b.SC(_)

            def _(e, f=f, hb=hb, sp=sp, ab=ab):
                b.w(e, f"sinA{sp}", f"acf{ab}")
                i = e.scalar_tensor_tensor(z["qvb"][sp][:], z["asb"][sp][:],
                                           z["cfs"][:, f:f + 1],
                                           hid[hb][0:128, 0:256],
                                           ALU.mult, ALU.add)
                b.inc(e, i, f"hidf{hb}")
                i = e.tensor_scalar(z["acb"][ab][:], z["qvb"][sp][:],
                                    z["b1s"][:, f:f + 1], 0.0,
                                    ALU.add, ALU.max)
                b.inc(e, i, f"acd{ab}")
            b.V(_)

            def _(e, f=f, ab=ab):
                b.w(e, f"acd{ab}")
                if f == 0:
                    b.w(e, "y2f0", "y2f1", "y2f2", "y2f3")
                for k in range(2):
                    for half in range(2):
                        i = e.matmul(y2bank[k][half][0:128, 0:512],
                                     z["acb"][ab][:, k * 128:k * 128 + 128],
                                     z["w2s"][:, f * 1024 + half * 512:
                                             f * 1024 + half * 512 + 512],
                                     start=(f == 0), stop=(f == NF - 1))
                b.inc(e, i, f"acf{ab}")
                if f == NF - 1:
                    b.inc(e, i, "y2d")
            b.T(_)

            if f + 3 < NF:
                hid_sec(f + 3)

        for k in range(2):
            oi = 2 * gg + k

            def _(e, k=k):
                b.w(e, "y2d", "y2bf")
                for half in range(2):
                    i = e.tensor_tensor(
                        z["y2b"][k][:, half * 512:half * 512 + 512],
                        y2bank[k][half][0:128, 0:512],
                        z["b2s"][:, half * 512:half * 512 + 512], ALU.add)
                b.inc(e, i, "y2ev")
                b.inc(e, i, f"y2f{2*k}")
                b.inc(e, i, f"y2f{2*k+1}")
            b.V(_)

            ln_stats(b, z, z["y2b"][k][:], k, "ln3S", wait=("y2ev",),
                     mean_dst=z["mn3"][:, oi:oi + 1],
                     nv_dst=z["nv3"][:, oi:oi + 1], rstd=False)

            def _(e, k=k, oi=oi):
                b.w(e, "ln3S")
                b.inc(e, e.dma_start(z["y2s_t"][oi], z["y2b"][k][:]), "y2w", 16)
            b.Y(_)

        def _(e):
            b.w(e, "y2w")
            b.inc(e, e.nop(), "y2bf")
        b.Y(_)


def emit_phaseFIN(b, z):
    def _(e):
        b.w(e, "ln3S")
        i1 = e.activation(z["m3o"][:], z["nv3"][:], AF.Ln,
                          bias=z["bts"][:, NB:NB + 1], scale=-1.0)
        b.inc(e, i1, "actf")
        b.w(e, "actf")
        i = e.activation(z["r3o"][:], z["m3o"][:], AF.Exp, scale=-0.5)
        b.inc(e, i, "r3A")
    b.SC(_)

    def _(e):
        b.w(e, "r3A")
        i = e.tensor_tensor(z["m3o"][:], z["mn3"][:], z["r3o"][:], ALU.mult)
        b.inc(e, i, "r3d")
    b.V(_)

    for oi in range(NOWN):
        k = oi % 2

        def _(e, oi=oi, k=k):
            b.w(e, "finf")
            b.inc(e, e.dma_start(z["xin"][0][:], z["x1s_t"][oi]), "fin", 16)
            b.inc(e, e.dma_start(z["y2b"][k][:], z["y2s_t"][oi]), "fin", 16)
        b.Y(_)

        def _(e, oi=oi, k=k):
            b.w(e, "fin", "r3d", "outd")
            i = e.tensor_scalar(z["ysb"][:], z["y2b"][k][:],
                                z["r3o"][:, oi:oi + 1], z["m3o"][:, oi:oi + 1],
                                ALU.mult, ALU.subtract)
            b.inc(e, i, "tsd")
        b.V(_)

        def _(e, oi=oi, k=k):
            b.w(e, "tsd", "out")
            i = e.tensor_add(z["osb"][k][:], z["ysb"][:], z["xin"][0][:])
            b.inc(e, i, "outd")
            b.inc(e, i, "finf")
        b.G(_)

        def _(e, oi=oi, k=k):
            b.w(e, "outd")
            b.inc(e, e.dma_start(z["yout_t"][oi], z["osb"][k][:]), "out", 16)
        b.Y(_)

    def _(e):
        b.w(e, "out")
    b.Y(_)


_NC_CACHE = [None]


def kernel(L0_attn_x, L0_attn_Wq, L0_attn_Wk, L0_attn_Wv, L0_attn_Wo,
           L0_ffn_W1, L0_ffn_b1, L0_ffn_quantum_phases,
           L0_ffn_quantum_weights_real, L0_ffn_quantum_weights_imag,
           L0_ffn_W2, L0_ffn_b2):
    X = np.ascontiguousarray(np.asarray(L0_attn_x, dtype=np.float32))
    wqT = np.ascontiguousarray(np.asarray(L0_attn_Wq, np.float32).T)
    wkT = np.ascontiguousarray(np.asarray(L0_attn_Wk, np.float32).T)
    wvT = np.ascontiguousarray(np.asarray(L0_attn_Wv, np.float32).T)
    woT = np.ascontiguousarray(np.asarray(L0_attn_Wo, np.float32).T)
    w1 = np.ascontiguousarray(np.asarray(L0_ffn_W1, np.float32))
    w2 = np.ascontiguousarray(np.asarray(L0_ffn_W2, np.float32))
    b1 = np.asarray(L0_ffn_b1, np.float32)
    b2 = np.asarray(L0_ffn_b2, np.float32)
    th = np.asarray(L0_ffn_quantum_phases, np.float32)
    wr = np.asarray(L0_ffn_quantum_weights_real, np.float32)
    wi = np.asarray(L0_ffn_quantum_weights_imag, np.float32)

    fm = lambda v: np.ascontiguousarray(v.reshape(32, 128).T)
    b2bc = np.ascontiguousarray(np.broadcast_to(b2[None, :], (128, DM)))

    ident = np.eye(128, dtype=np.float32)
    cbf, btabs = {}, {}
    for p in range(2):
        me, mo = host_masks(p)
        cbx = np.zeros((128, 512), dtype=np.float32)
        cbx[:, 0:128] = ident
        cbx[:, 128:256] = me
        cbx[:, 256:384] = mo
        cbx[:, 384:512] = 1.0
        cbf[p] = cbx.astype(ml_dtypes.bfloat16)
        bt = host_bias_table(p)
        btabs[p] = np.ascontiguousarray(
            np.concatenate([bt, np.full((128, 1), EPS, np.float32)], axis=1))

    in_maps = []
    for core in range(8):
        s, p = core // 2, core % 2
        xs = X[s]
        own = np.ascontiguousarray(
            xs.reshape(NT, 128, DM)[p::2].reshape(NOWN * 128, DM))
        in_maps.append({
            "xkv": xs, "xqo": own,
            "wqT": wqT, "wkT": wkT, "wvT": wvT, "woT": woT,
            "w1": w1, "w2": w2,
            "thfm": fm(th), "b1fm": fm(b1), "wrfm": fm(wr), "wifm": fm(wi),
            "b2bc": b2bc, "btab": btabs[p], "cbf": cbf[p],
        })

    if _NC_CACHE[0] is None:
        _NC_CACHE[0] = build_nc()
    res = run_bass_kernel_spmd(_NC_CACHE[0], in_maps, core_ids=list(range(8)))

    out = np.empty((NSEQ, T, DM), dtype=np.float32)
    for core in range(8):
        s, p = core // 2, core % 2
        yc = np.asarray(res.results[core]["y"]).reshape(NOWN, 128, DM)
        out[s].reshape(NT, 128, DM)[p::2] = yc
    return out


if __name__ == "__main__":
    print(f"NB={NB}")
    print("building...")
    nc = build_nc()
    print("built ok")
